# version 11
# baseline (speedup 1.0000x reference)
import numpy as np
import ml_dtypes
from contextlib import ExitStack

import concourse.bass as bass
import concourse.mybir as mybir
from concourse.bass_utils import run_bass_kernel_spmd

F32 = mybir.dt.float32
BF16 = mybir.dt.bfloat16
AF = mybir.ActivationFunctionType
ALU = mybir.AluOpType
AX = mybir.AxisListType
BF = ml_dtypes.bfloat16

D = 1024
T = 4096
NS = 16
E = 8
DC = 8
NT = 32
TC = 32
LN = TC + 1
NCH = T // TC
GRP = 512
NG = T // GRP
EPS = 1e-5

N_GU = E * 64 + 16
N_DN = E * 32 + 8

EXPERTS = [(e * 64, 32, e * 32, 32, False) for e in range(E)] + \
          [(E * 64, 8, E * 32, 8, True)]
JJ_PER_G = sum(nf for _, nf, _, _, _ in EXPERTS)
DN_PER_G = sum(nd for _, _, _, nd, _ in EXPERTS)



def _t(x):
    return x.tensor if isinstance(x, bass.AP) else x

def build_nc(debug=False, phases=5):
    nc = bass.Bass("TRN2", target_bir_lowering=False)

    x_h = nc.dram_tensor("x", [T, D], F32, kind="ExternalInput").ap()
    a_lane_h = nc.dram_tensor("a_lane", [128, 128], F32, kind="ExternalInput").ap()
    wdt_h = nc.dram_tensor("wdt_fm", [128, 8], F32, kind="ExternalInput").ap()
    bdt_h = nc.dram_tensor("bdt_fm", [128, 8], F32, kind="ExternalInput").ap()
    wxpt_h = nc.dram_tensor("wxpt_fm", [128, 256], F32, kind="ExternalInput").ap()
    bxp_h = nc.dram_tensor("bxp_col", [32, 1], F32, kind="ExternalInput").ap()
    woutt_h = nc.dram_tensor("woutt_fm", [128, 8192], F32, kind="ExternalInput").ap()
    bout_h = nc.dram_tensor("bout_fm", [128, 8], F32, kind="ExternalInput").ap()
    gam_h = nc.dram_tensor("gam_rep", [128, D], F32, kind="ExternalInput").ap()
    bet_h = nc.dram_tensor("bet_rep", [128, D], F32, kind="ExternalInput").ap()
    rw_h = nc.dram_tensor("rw_fm", [128, 64], F32, kind="ExternalInput").ap()
    gu_h = nc.dram_tensor("gu_tiled", [N_GU, 128, 1024], F32, kind="ExternalInput").ap()
    dn_h = nc.dram_tensor("dn_tiled", [N_DN, 128, 1024], F32, kind="ExternalInput").ap()
    idb_h = nc.dram_tensor("ident_b", [128, 128], BF16, kind="ExternalInput").ap()
    idf_h = nc.dram_tensor("ident_f", [128, 128], F32, kind="ExternalInput").ap()
    ones_h = nc.dram_tensor("ones_f", [1, 128], F32, kind="ExternalInput").ap()

    out_h = nc.dram_tensor("out", [T, D], F32, kind="ExternalOutput").ap()

    xln_s = nc.dram_tensor("xln_s", [DC, 128, T], BF16, kind="Internal").ap()
    xn_sc = nc.dram_tensor("xn_sc", [NCH, 128, 8 * TC], BF16, kind="Internal").ap()
    bc_s = nc.dram_tensor("bc_s", [32, T], F32, kind="Internal").ap()
    u_s = nc.dram_tensor("u_s", [1, T], F32, kind="Internal").ap()
    gu_sp = nc.dram_tensor("gu_sp", [N_GU, 128, 1024], BF16, kind="Internal").ap()
    dn_sp = nc.dram_tensor("dn_sp", [N_DN, 128, 1024], BF16, kind="Internal").ap()
    s_s = nc.dram_tensor("s_s", [DC, 128, T], BF16, kind="Internal").ap()

    dbg = {}
    if debug:
        dbg["xn"] = nc.dram_tensor("dbg_xn", [128, DC * T], BF16, kind="ExternalOutput").ap()
        dbg["bc"] = nc.dram_tensor("dbg_bc", [32, T], F32, kind="ExternalOutput").ap()
        dbg["comb"] = nc.dram_tensor("dbg_comb", [E, T], F32, kind="ExternalOutput").ap()
        dbg["s"] = nc.dram_tensor("dbg_s", [DC * 128, T], BF16, kind="ExternalOutput").ap()
        dbg["u"] = nc.dram_tensor("dbg_u", [1, T], F32, kind="ExternalOutput").ap()
        dbg["xln"] = nc.dram_tensor("dbg_xln", [DC * 128, T], BF16, kind="ExternalOutput").ap()
        dbg["y"] = nc.dram_tensor("dbg_y", [128, DC * GRP], BF16, kind="ExternalOutput").ap()

    with ExitStack() as stk:
        sbA = lambda n, s, d: stk.enter_context(nc.sbuf_tensor(n, s, d))
        a_lane = sbA("a_lane_sb", [128, 128], F32)
        wdt = sbA("wdt_sb", [128, 8], F32)
        bdt = sbA("bdt_sb", [128, 8], F32)
        bxp = sbA("bxp_sb", [32, 1], F32)
        bout = sbA("bout_sb", [128, 8], F32)
        idb = sbA("idb_sb", [128, 128], BF16)
        idf = sbA("idf_sb", [128, 128], F32)
        ones_f = sbA("ones_sb", [1, 128], F32)
        c_eps = sbA("c_eps", [128, 1], F32)
        c_one = sbA("c_one", [128, 1], F32)
        c_ten = sbA("c_ten", [128, 1], F32)
        c_nten = sbA("c_nten", [128, 1], F32)
        comb_sb = sbA("comb_sb", [128, NT * 8], F32)

        if phases < 1:
            return nc
        with nc.semaphore("c_lds") as c_lds, nc.Block() as blk:
            @blk.sync
            def _(sync):
                for dst, src in [(a_lane[:], a_lane_h), (wdt[:], wdt_h),
                                 (bdt[:], bdt_h), (bxp[:], bxp_h),
                                 (bout[:], bout_h), (idb[:], idb_h),
                                 (idf[:], idf_h), (ones_f[:], ones_h)]:
                    sync.dma_start(out=dst, in_=src).then_inc(c_lds, 16)

            @blk.vector
            def _(vector):
                vector.memset(c_eps[:], EPS)
                vector.memset(c_one[:], 1.0)
                vector.memset(c_ten[:], 10.0)
                vector.memset(c_nten[:], -10.0)
                vector.drain()

            @blk.scalar
            def _(scalar):
                scalar.wait_ge(c_lds, 16 * 8)
                scalar.activation(out=a_lane[:], in_=a_lane[:], func=AF.Exp)
                scalar.drain()
                scalar.mul(out=a_lane[:], in_=a_lane[:], mul=-1.0)
                scalar.drain()

        with ExitStack() as stkA:
            sbB = lambda n, s, d: stkA.enter_context(nc.sbuf_tensor(n, s, d))
            woutt_b = sbB("woutt_b", [128, 8192], BF16)

            if phases < 2:
                return nc
            with ExitStack() as st0:
                sm0 = lambda n: st0.enter_context(nc.semaphore(n))
                p_w, p_x0, p_x1 = sm0("p_w"), sm0("p_x0"), sm0("p_x1")
                p_sq, p_va, p_sc = sm0("p_sq"), sm0("p_va"), sm0("p_sc")
                p_vb, p_sc2, p_rdy = sm0("p_vb"), sm0("p_sc2"), sm0("p_rdy")
                p_tp, p_ev, p_mm = sm0("p_tp"), sm0("p_ev"), sm0("p_mm")
                p_sm, p_q = sm0("p_sm"), sm0("p_q")
                p_cbS = sm0("p_cbS")
                p_stu, p_stx, p_stf = sm0("p_stu"), sm0("p_stx"), sm0("p_stf")
                p_stn = sm0("p_stn")
                blk = st0.enter_context(nc.Block())
                sb0 = lambda n, s, d: st0.enter_context(nc.sbuf_tensor(n, s, d))
                xb = [sb0("xb%d" % i, [128, D], F32) for i in range(2)]
                gam = sb0("gam", [128, D], F32)
                bet = sb0("bet", [128, D], F32)
                rw_f = sb0("rw_f", [128, 64], F32)
                wxpt_f = sb0("wxpt_f", [128, 256], F32)
                rw_b = sb0("rw_b", [128, 64], BF16)
                wxpt_b = sb0("wxpt_b", [128, 256], BF16)
                sx = sb0("sx", [128, 1], F32)
                sx2 = sb0("sx2", [128, 1], F32)
                mu = sb0("mu", [128, 1], F32)
                var = sb0("var", [128, 1], F32)
                vtmp = sb0("vtmp", [128, 1], F32)
                vtmp2 = sb0("vtmp2", [128, 1], F32)
                rstd = sb0("rstd", [128, 1], F32)
                rstd2 = sb0("rstd2", [128, 1], F32)
                xn_f = sb0("xn_f", [128, D], F32)
                xn_t = sb0("xn_t", [128, D], BF16)
                xln_a = sb0("xln_a", [128, D], F32)
                xln_t = sb0("xln_t", [128, D], BF16)
                xlf = sb0("xlf", [128, DC * 128], BF16)
                xnf_st = sb0("xnf_st", [128, DC * 128], BF16)
                u_col = sb0("u_col", [128, 1], F32)
                sq_t = sb0("sq_t", [128, D], F32)
                q_t = sb0("q_t", [128, 8], F32)
                m1 = sb0("m1", [128, 1], F32)
                m1n = sb0("m1n", [128, 1], F32)
                qm = sb0("qm", [128, 8], F32)
                m2 = sb0("m2", [128, 1], F32)
                den = sb0("den", [128, 1], F32)
                rden = sb0("rden", [128, 1], F32)
                mask = sb0("mask", [128, 8], F32)
                bc_fm = sb0("bc_fm", [32, T], F32)
                ps_t = [st0.enter_context(nc.psum_tensor("ps_t%d" % i, [128, 128], BF16))
                        for i in range(4)]
                ps_r = st0.enter_context(nc.psum_tensor("ps_r", [128, 8], F32))
                ps_bc = st0.enter_context(nc.psum_tensor("ps_bc", [32, 128], F32))

                p_x = [p_x0, p_x1]

                @blk.sync
                def _(sync):
                    for dst, src in [(gam[:], gam_h), (bet[:], bet_h),
                                     (rw_f[:], rw_h), (wxpt_f[:], wxpt_h)]:
                        sync.dma_start(out=dst, in_=src).then_inc(p_w, 16)
                    for k in range(NT):
                        if k >= 2:
                            sync.wait_ge(p_rdy, k - 1)
                        sync.dma_start(out=xb[k % 2][:],
                                       in_=x_h[k * 128:(k + 1) * 128, :]
                                       ).then_inc(p_x[k % 2], 16)

                @blk.scalar
                def _(scalar):
                    scalar.wait_ge(p_w, 16 * 4)
                    scalar.copy(out=rw_b[:], in_=rw_f[:])
                    scalar.copy(out=wxpt_b[:], in_=wxpt_f[:])
                    scalar.drain()
                    scalar.nop().then_inc(p_sq, 1)
                    for k in range(NT):
                        xk = xb[k % 2]
                        scalar.wait_ge(p_x[k % 2], 16 * (k // 2 + 1))
                        scalar.activation(out=sq_t[:], in_=xk[:], func=AF.Square,
                                          accum_out=sx2[:]).then_inc(p_sq, 1)
                        scalar.wait_ge(p_va, k + 1)
                        scalar.activation(out=vtmp[:], in_=var[:], func=AF.Ln,
                                          bias=c_eps[:])
                        scalar.drain()
                        scalar.activation(out=rstd[:], in_=vtmp[:], func=AF.Exp,
                                          scale=-0.5).then_inc(p_sc, 1)
                        scalar.wait_ge(p_vb, k + 1)
                        scalar.activation(out=vtmp2[:], in_=vtmp2[:], func=AF.Ln,
                                          bias=c_eps[:])
                        scalar.drain()
                        scalar.activation(out=rstd2[:], in_=vtmp2[:], func=AF.Exp,
                                          scale=-0.5).then_inc(p_sc2, 1)
                        scalar.wait_ge(p_sm, 2 * k + 1)
                        scalar.activation(out=q_t[:], in_=ps_r[:], func=AF.Exp,
                                          bias=m1n[:]).then_inc(p_q, 1)
                        scalar.wait_ge(p_mm, 2 * k + 2)
                        scalar.activation(out=bc_fm[:, k * 128:(k + 1) * 128],
                                          in_=ps_bc[:], func=AF.Identity,
                                          bias=bxp[:]).then_inc(p_cbS, 1)

                @blk.vector
                def _(vector):
                    for k in range(NT):
                        xk = xb[k % 2]
                        vector.wait_ge(p_x[k % 2], 16 * (k // 2 + 1))
                        vector.wait_ge(p_sq, k + 2)
                        vector.tensor_reduce(out=sx[:], in_=xk[:], axis=AX.X,
                                             op=ALU.add)
                        vector.drain()
                        vector.tensor_scalar_mul(out=mu[:], in0=sx[:],
                                                 scalar1=1.0 / D)
                        vector.drain()
                        vector.tensor_mul(out=vtmp[:], in0=mu[:], in1=mu[:])
                        vector.tensor_scalar_mul(out=var[:], in0=sx2[:],
                                                 scalar1=1.0 / D)
                        vector.drain()
                        vector.tensor_sub(out=var[:], in0=var[:],
                                          in1=vtmp[:]).then_inc(p_va, 1)
                        vector.wait_ge(p_sc, k + 1)
                        vector.tensor_mul(out=vtmp2[:], in0=var[:], in1=rstd[:])
                        vector.drain()
                        vector.tensor_mul(out=vtmp2[:], in0=vtmp2[:],
                                          in1=rstd[:]).then_inc(p_vb, 1)
                        if k >= 1:
                            vector.wait_ge(p_tp, 16 * k)
                            vector.wait_ge(p_stu, 16 * k)
                        vector.tensor_scalar(out=xn_f[:], in0=xk[:], scalar1=mu[:],
                                             scalar2=rstd[:], op0=ALU.subtract,
                                             op1=ALU.mult)
                        vector.drain()
                        vector.tensor_copy(out=xn_t[:], in_=xn_f[:])
                        vector.tensor_copy(out=u_col[:], in_=xn_f[:, 0:1])
                        vector.wait_ge(p_sc2, k + 1)
                        vector.scalar_tensor_tensor(out=xln_a[:], in0=xn_f[:],
                                                    scalar=rstd2[:], in1=gam[:],
                                                    op0=ALU.mult, op1=ALU.mult)
                        vector.drain()
                        vector.tensor_add(out=xln_t[:], in0=xln_a[:],
                                          in1=bet[:]).then_inc(p_rdy, 1)
                        if k >= 1:
                            vector.wait_ge(p_stx, 16 * 8 * k)
                            vector.wait_ge(p_stn, 16 * 8 * k)
                        for c in range(DC):
                            for which in range(2):
                                n = 16 * k + 2 * c + which + 1
                                vector.wait_ge(p_tp, n)
                                pst = ps_t[(2 * c + which) % 4]
                                if which == 0:
                                    vector.tensor_copy(
                                        out=xnf_st[:, c * 128:(c + 1) * 128],
                                        in_=pst[:]).then_inc(p_ev, 1)
                                else:
                                    vector.tensor_copy(
                                        out=xlf[:, c * 128:(c + 1) * 128],
                                        in_=pst[:]).then_inc(p_ev, 1)
                        vector.wait_ge(p_mm, 2 * k + 1)
                        vector.tensor_reduce(out=m1[:], in_=ps_r[:], axis=AX.X,
                                             op=ALU.max)
                        vector.drain()
                        vector.tensor_scalar_mul(out=m1n[:], in0=m1[:],
                                                 scalar1=-1.0).then_inc(p_sm, 1)
                        vector.wait_ge(p_q, k + 1)
                        vector.tensor_scalar(out=mask[:], in0=q_t[:],
                                             scalar1=0.999999, scalar2=None,
                                             op0=ALU.is_ge)
                        vector.drain()
                        vector.scalar_tensor_tensor(out=qm[:], in0=mask[:],
                                                    scalar=-2.0, in1=q_t[:],
                                                    op0=ALU.mult, op1=ALU.add)
                        vector.drain()
                        vector.tensor_reduce(out=m2[:], in_=qm[:], axis=AX.X,
                                             op=ALU.max)
                        vector.drain()
                        vector.tensor_scalar_add(out=den[:], in0=m2[:],
                                                 scalar1=1.0)
                        vector.drain()
                        vector.reciprocal(out=rden[:], in_=den[:])
                        vector.tensor_scalar(out=mask[:], in0=q_t[:],
                                             scalar1=m2[:], scalar2=None,
                                             op0=ALU.is_ge)
                        vector.drain()
                        cslice = comb_sb[:, k * 8:(k + 1) * 8]
                        vector.tensor_mul(out=cslice, in0=q_t[:], in1=mask[:])
                        vector.drain()
                        vector.tensor_scalar_mul(out=cslice, in0=cslice,
                                                 scalar1=rden[:]).then_inc(p_sm, 1)
                    if debug:
                        vector.drain()

                @blk.tensor
                def _(tensor):
                    tensor.wait_ge(p_sq, 1)
                    for k in range(NT):
                        tensor.wait_ge(p_rdy, k + 1)
                        for c in range(DC):
                            for which in range(2):
                                n = 16 * k + 2 * c + which + 1
                                if n > 4:
                                    tensor.wait_ge(p_ev, n - 4)
                                pst = ps_t[(2 * c + which) % 4]
                                src = xn_t if which == 0 else xln_t
                                tensor.transpose(out=pst[:],
                                                 in_=src[:, c * 128:(c + 1) * 128],
                                                 identity=idb[:]).then_inc(p_tp, 1)
                        tensor.wait_ge(p_ev, 16 * k + 16)
                        if k >= 1:
                            tensor.wait_ge(p_q, k)
                        for c in range(DC):
                            mm = tensor.matmul(out=ps_r[:],
                                               lhsT=xlf[:, c * 128:(c + 1) * 128],
                                               rhs=rw_b[:, c * 8:(c + 1) * 8],
                                               start=(c == 0), stop=(c == DC - 1))
                        mm.then_inc(p_mm, 1)
                        if k >= 1:
                            tensor.wait_ge(p_cbS, k)
                        for c in range(DC):
                            mm = tensor.matmul(out=ps_bc[:],
                                               lhsT=wxpt_b[:, c * 32:(c + 1) * 32],
                                               rhs=xnf_st[:, c * 128:(c + 1) * 128],
                                               start=(c == 0), stop=(c == DC - 1))
                        mm.then_inc(p_mm, 1)

                @blk.gpsimd
                def _(g):
                    for k in range(NT):
                        g.wait_ge(p_rdy, k + 1)
                        g.dma_start(out=u_s[0:1, k * 128:(k + 1) * 128],
                                    in_=u_col[:]).then_inc(p_stu, 16)
                        g.wait_ge(p_ev, 16 * k + 16)
                        for c in range(DC):
                            g.dma_start(out=bass.AP(_t(xln_s),
                                                    c * 128 * T + k * 128,
                                                    [[T, 128], [1, 128]]),
                                        in_=xlf[:, c * 128:(c + 1) * 128]
                                        ).then_inc(p_stx, 16)
                        for c in range(DC):
                            g.dma_start(
                                out=bass.AP(_t(xn_sc),
                                            (4 * k) * (128 * 8 * TC) + c * TC,
                                            [[8 * TC, 128], [128 * 8 * TC, 4],
                                             [1, TC]]),
                                in_=xnf_st[:, c * 128:(c + 1) * 128]
                            ).then_inc(p_stn, 16)
                    g.wait_ge(p_cbS, NT)
                    g.dma_start(out=bc_s, in_=bc_fm[:]).then_inc(p_stf, 16)
                    nst = 1
                    if debug:
                        g.dma_start(out=dbg["bc"], in_=bc_fm[:]).then_inc(p_stf, 16)
                        g.dma_start(out=dbg["comb"], in_=comb_sb[:]).then_inc(p_stf, 16)
                        g.dma_start(out=dbg["xn"],
                                    in_=bass.AP(_t(xn_sc), 0,
                                                [[8 * TC, 128],
                                                 [128 * 8 * TC, NCH], [1, 8 * TC]])
                                    ).then_inc(p_stf, 16)
                        nst += 3
                    g.wait_ge(p_stu, 16 * NT)
                    g.wait_ge(p_stx, 16 * 8 * NT)
                    g.wait_ge(p_stn, 16 * 8 * NT)
                    g.wait_ge(p_stf, 16 * nst)

            if phases < 4:
                return nc
            with ExitStack() as st2:
                sm2 = lambda n: st2.enter_context(nc.semaphore(n))
                s_stg = [sm2("s_stg%d" % q) for q in range(2)]
                s_ur = [sm2("s_ur%d" % q) for q in range(4)]
                s_sst = [sm2("s_sst%d" % q) for q in range(DC)]
                s_bc, s_dt1, s_sp = sm2("s_bc"), sm2("s_dt1"), sm2("s_sp")
                s_dtc, s_dt2, s_da = sm2("s_dtc"), sm2("s_dt2"), sm2("s_da")
                s_vec, s_y, s_wo = sm2("s_vec"), sm2("s_y"), sm2("s_wo")
                s_sev = sm2("s_sev")
                s_xn = [sm2("s_xn%d" % q) for q in range(2)]
                s_zc = sm2("s_zc")
                w_in = [sm2("w_in%d" % q) for q in range(8)]
                w_sp = [sm2("w_sp%d" % q) for q in range(8)]
                w_cse = sm2("w_cse")
                slabs = [(2, i) for i in range(8)] + \
                        [(0, i) for i in range(N_GU)] + \
                        [(1, i) for i in range(N_DN)]
                NSL = len(slabs)
                slab_of_chunk = [
                    list(range((NSL * c) // NCH, (NSL * (c + 1)) // NCH))
                    for c in range(NCH)]
                spill_ord = {}
                _spq = [0] * 8
                for _i, (_k, _x) in enumerate(slabs):
                    if _k != 2:
                        _spq[_i % 8] += 1
                        spill_ord[_i] = _spq[_i % 8]
                s_dbg = sm2("s_dbg")
                blk = st2.enter_context(nc.Block())
                sb2 = lambda n, s, d: st2.enter_context(nc.sbuf_tensor(n, s, d))
                fb = [sb2("fb%d" % i, [128, 1024], F32) for i in range(8)]
                bb = [sb2("bb%d" % i, [128, 1024], BF16) for i in range(8)]
                dAarg = sb2("dAarg", [128, 128 * TC], F32)
                dAb = [sb2("dAb%d" % i, [128, 128 * LN], F32) for i in range(2)]
                zB = sb2("zB", [128, 128 * LN], F32)
                hB = sb2("hB", [128, 128 * LN], F32)
                urep = [sb2("urep%d" % i, [128, TC], F32) for i in range(4)]
                dtp = [sb2("dtp%d" % i, [128, 8 * TC], F32) for i in range(2)]
                sp_a = [sb2("sp_a%d" % i, [128, 8 * TC], F32) for i in range(2)]
                dtc = [sb2("dtc%d" % i, [128, 8 * TC], F32) for i in range(2)]
                xnc = [sb2("xnc%d" % i, [128, 8 * TC], BF16) for i in range(2)]
                zc = [sb2("zc%d" % i, [128, 8 * TC], F32) for i in range(2)]
                brow = [sb2("brow%d" % i, [1, NS * TC], F32) for i in range(2)]
                crow = [sb2("crow%d" % i, [1, NS * TC], F32) for i in range(2)]
                y_st = [sb2("y_st%d" % i, [128, DC * GRP], BF16) for i in range(2)]
                s_tile = [sb2("s_tile%d" % i, [128, GRP], BF16) for i in range(DC)]
                ps_b = [st2.enter_context(nc.psum_tensor("ps_b%d" % i, [128, NS * TC], F32))
                        for i in range(2)]
                ps_c = [st2.enter_context(nc.psum_tensor("ps_c%d" % i, [128, NS * TC], F32))
                        for i in range(2)]
                ps_wo = [st2.enter_context(nc.psum_tensor("ps_wo%d" % i, [128, GRP], F32))
                         for i in range(2)]

                def lane_ap(t, off):
                    return bass.AP(_t(t), off,
                                   [[128 * LN, 128], [16 * LN, 8], [LN, 16], [1, TC]])

                def slot0_ap(t, off=0):
                    return bass.AP(_t(t), off, [[128 * LN, 128], [LN, 128], [1, 1]])

                def flat_ap(t):
                    return bass.AP(_t(t), 0, [[128 * LN, 128], [1, 128 * LN]])

                def cmp_ap(t, nrep=False):
                    return bass.AP(_t(t), 0,
                                   [[8 * TC, 128], [TC, 8], [0, 16], [1, TC]])

                @blk.sync
                def _(sync):
                    for c in range(NCH):
                        pp = c % 2
                        if c >= 2:
                            sync.wait_ge(s_bc, c - 1)
                        sync.dma_start(out=brow[pp][:],
                                       in_=bass.AP(_t(bc_s), c * TC,
                                                   [[NS * TC, 1], [T, NS], [1, TC]])
                                       ).then_inc(s_stg[pp], 16)
                        sync.dma_start(out=crow[pp][:],
                                       in_=bass.AP(_t(bc_s), NS * T + c * TC,
                                                   [[NS * TC, 1], [T, NS], [1, TC]])
                                       ).then_inc(s_stg[pp], 16)
                        for cu in ([0, 1] if c == 0 else []) + ([c + 2] if c + 2 < NCH else []):
                            if cu >= 4:
                                sync.wait_ge(s_dt1, cu - 1)
                            sync.dma_start(out=urep[cu % 4][:],
                                           in_=u_s[0:1, cu * TC:(cu + 1) * TC]
                                           .partition_broadcast(128)
                                           ).then_inc(s_ur[cu % 4], 16)
                        if c >= 2:
                            sync.wait_ge(s_zc, c - 1)
                        sync.dma_start(out=xnc[pp][:],
                                       in_=xn_sc[c]).then_inc(s_xn[pp], 16)
                        if c >= 2 and (c - 2) % 16 == 15:
                            gi = (c - 2) // 16
                            for oc in range(DC):
                                n = gi * 8 + oc
                                sync.wait_ge(s_sev, n + 1)
                                sync.dma_start(
                                    out=bass.AP(_t(s_s), oc * 128 * T + gi * GRP,
                                                [[T, 128], [1, GRP]]),
                                    in_=s_tile[oc][:]).then_inc(s_sst[oc], 16)
                    for oc in range(DC):
                        n = (NG - 1) * 8 + oc
                        sync.wait_ge(s_sev, n + 1)
                        sync.dma_start(out=bass.AP(_t(s_s),
                                                   oc * 128 * T + (NG - 1) * GRP,
                                                   [[T, 128], [1, GRP]]),
                                       in_=s_tile[oc][:]).then_inc(s_sst[oc], 16)
                    for oc in range(DC):
                        sync.wait_ge(s_sst[oc], 16 * NG)

                @blk.tensor
                def _(tensor):
                    for c in range(NCH):
                        pp = c % 2
                        tensor.wait_ge(s_stg[pp], 32 * (c // 2 + 1))
                        if c >= 2:
                            tensor.wait_ge(s_vec, c - 1)
                        tensor.matmul(out=ps_b[pp][:], lhsT=ones_f[:],
                                      rhs=brow[pp][:], start=True, stop=True)
                        tensor.matmul(out=ps_c[pp][:], lhsT=ones_f[:],
                                      rhs=crow[pp][:], start=True,
                                      stop=True).then_inc(s_bc, 1)
                        for gi in ([c // 16 - 1] if (c % 16 == 1 and c > 16)
                                   else ([NG - 1] if c == NCH - 1 else [])):
                            gp = gi % 2
                            if gi == 0:
                                tensor.wait_ge(w_cse, 8)
                            tensor.wait_ge(s_y, (gi + 1) * 16)
                            for oc in range(DC):
                                op = oc % 2
                                n = gi * 8 + oc
                                if n >= 2:
                                    tensor.wait_ge(s_sev, n - 1)
                                for dc in range(DC):
                                    mm = tensor.matmul(
                                        out=ps_wo[op][:],
                                        lhsT=woutt_b[:, dc * 1024 + oc * 128:
                                                     dc * 1024 + (oc + 1) * 128],
                                        rhs=y_st[gp][:, dc * GRP:(dc + 1) * GRP],
                                        start=(dc == 0), stop=(dc == DC - 1))
                                mm.then_inc(s_wo, 1)

                @blk.gpsimd
                def _(g):
                    def slab_src(kind, idx):
                        if kind == 0:
                            return gu_h[idx]
                        if kind == 1:
                            return dn_h[idx]
                        return woutt_h[:, idx * 1024:(idx + 1) * 1024]

                    def spills(batch):
                        for i in batch:
                            kind, idx = slabs[i]
                            if kind == 2:
                                continue
                            g.wait_ge(w_cse, i + 1)
                            dst = gu_sp if kind == 0 else dn_sp
                            g.dma_start(out=dst[idx],
                                        in_=bb[i % 8][:]).then_inc(w_sp[i % 8], 16)

                    for c in range(NCH):
                        pp = c % 2
                        if c >= 1:
                            spills(slab_of_chunk[c - 1])
                        for i in slab_of_chunk[c]:
                            if i >= 8:
                                g.wait_ge(w_cse, i - 7)
                            g.dma_start(out=fb[i % 8][:],
                                        in_=slab_src(*slabs[i])
                                        ).then_inc(w_in[i % 8], 16)
                        g.wait_ge(s_dtc, c + 1)
                        g.wait_ge(s_xn[pp], 16 * (c // 2 + 1))
                        if c >= 2:
                            g.wait_ge(s_vec, c - 1)
                        g.tensor_mul(out=zc[pp][:], in0=dtc[pp][:],
                                     in1=xnc[pp][:]).then_inc(s_zc, 1)
                        if c >= 1:
                            g.wait_ge(s_da, c)
                        g.tensor_tensor(out=bass.AP(_t(dAarg), 0,
                                                    [[128 * TC, 128], [16 * TC, 8],
                                                     [TC, 16], [1, TC]]),
                                        in0=bass.AP(_t(a_lane), 0,
                                                    [[128, 128], [16, 8], [1, 16],
                                                     [0, TC]]),
                                        in1=bass.AP(_t(dtc[pp]), 0,
                                                    [[8 * TC, 128], [TC, 8],
                                                     [0, 16], [1, TC]]),
                                        op=ALU.mult).then_inc(s_dt2, 1)
                    spills(slab_of_chunk[NCH - 1])
                    for q in range(8):
                        g.wait_ge(w_sp[q], 16 * _spq[q])

                @blk.scalar
                def _(scalar):
                    def softplus(k):
                        kp = k % 2
                        scalar.wait_ge(s_dt1, k + 1)
                        if k >= 2:
                            scalar.wait_ge(s_dtc, k - 1)
                        scalar.activation(out=sp_a[kp][:], in_=dtp[kp][:],
                                          func=AF.Abs)
                        scalar.drain()
                        scalar.activation(out=sp_a[kp][:], in_=sp_a[kp][:],
                                          func=AF.Exp, scale=-1.0)
                        scalar.drain()
                        scalar.activation(out=sp_a[kp][:], in_=sp_a[kp][:],
                                          func=AF.Ln,
                                          bias=c_one[:]).then_inc(s_sp, 1)

                    def casts(batch):
                        for i in batch:
                            kind, idx = slabs[i]
                            scalar.wait_ge(w_in[i % 8], 16 * (i // 8 + 1))
                            if kind == 2:
                                scalar.copy(out=woutt_b[:, idx * 1024:(idx + 1) * 1024],
                                            in_=fb[i % 8][:]).then_inc(w_cse, 1)
                            else:
                                if i - 8 in spill_ord:
                                    scalar.wait_ge(w_sp[i % 8],
                                                   16 * spill_ord[i - 8])
                                scalar.copy(out=bb[i % 8][:],
                                            in_=fb[i % 8][:]).then_inc(w_cse, 1)

                    softplus(0)
                    softplus(1)
                    for c in range(NCH):
                        pp = c % 2
                        if c >= 1:
                            casts(slab_of_chunk[c - 1])
                        if c == NCH - 1:
                            casts(slab_of_chunk[c])
                        scalar.wait_ge(s_dt2, c + 1)
                        if c >= 2:
                            scalar.wait_ge(s_vec, c - 1)
                        scalar.activation(out=lane_ap(dAb[pp], 1),
                                          in_=bass.AP(_t(dAarg), 0,
                                                      [[128 * TC, 128],
                                                       [16 * TC, 8], [TC, 16],
                                                       [1, TC]]),
                                          func=AF.Exp).then_inc(s_da, 1)
                        for gi in ([c // 16 - 1] if (c % 16 == 1 and c > 16)
                                   else ([NG - 1] if c == NCH - 1 else [])):
                            for oc in range(DC):
                                op = oc % 2
                                n = gi * 8 + oc
                                scalar.wait_ge(s_wo, n + 1)
                                if gi >= 1:
                                    scalar.wait_ge(s_sst[oc], 16 * gi)
                                scalar.activation(out=s_tile[oc][:],
                                                  in_=ps_wo[op][:], func=AF.Identity,
                                                  bias=bout[:, oc:oc + 1]
                                                  ).then_inc(s_sev, 1)
                        if c + 2 < NCH:
                            softplus(c + 2)

                @blk.vector
                def _(vector):
                    vector.memset(slot0_ap(dAb[0]), 0.0)
                    vector.memset(slot0_ap(dAb[1]), 0.0)
                    vector.memset(slot0_ap(zB), 0.0)
                    vector.drain()

                    def make_dtp(k):
                        kp = k % 2
                        vector.wait_ge(s_ur[k % 4], 16 * (k // 4 + 1))
                        if k >= 2:
                            vector.wait_ge(s_sp, k - 1)
                        for gg in range(DC):
                            ts = vector.tensor_scalar(
                                out=dtp[kp][:, gg * TC:(gg + 1) * TC],
                                in0=urep[k % 4][:],
                                scalar1=wdt[:, gg:gg + 1],
                                scalar2=bdt[:, gg:gg + 1],
                                op0=ALU.mult, op1=ALU.add)
                        ts.then_inc(s_dt1, 1)

                    def make_dtc(k):
                        kp = k % 2
                        vector.wait_ge(s_sp, k + 1)
                        if k >= 2:
                            vector.wait_ge(s_dt2, k - 1)
                        vector.scalar_tensor_tensor(out=dtc[kp][:], in0=dtp[kp][:],
                                                    scalar=0.0, in1=sp_a[kp][:],
                                                    op0=ALU.max,
                                                    op1=ALU.add).then_inc(s_dtc, 1)

                    make_dtp(0)
                    make_dtp(1)
                    make_dtc(0)
                    for c in range(NCH):
                        pp = c % 2
                        gi = c // 16
                        gp = gi % 2
                        if c + 1 < NCH:
                            make_dtc(c + 1)
                        if c + 2 < NCH:
                            make_dtp(c + 2)
                        if c > 0:
                            vector.tensor_copy(out=slot0_ap(zB),
                                               in_=slot0_ap(hB, TC))
                        vector.wait_ge(s_bc, c + 1)
                        vector.wait_ge(s_zc, c + 1)
                        vector.tensor_tensor(out=lane_ap(zB, 1),
                                             in0=cmp_ap(zc[pp]),
                                             in1=bass.AP(_t(ps_b[pp]), 0,
                                                         [[NS * TC, 128], [0, 8],
                                                          [TC, 16], [1, TC]]),
                                             op=ALU.mult)
                        vector.drain()
                        vector.wait_ge(s_da, c + 1)
                        vector.tensor_tensor_scan(out=flat_ap(hB),
                                                  data0=flat_ap(dAb[pp]),
                                                  data1=flat_ap(zB), initial=0.0,
                                                  op0=ALU.mult, op1=ALU.add)
                        vector.drain()
                        vector.tensor_tensor(out=lane_ap(zB, 1),
                                             in0=lane_ap(hB, 1),
                                             in1=bass.AP(_t(ps_c[pp]), 0,
                                                         [[NS * TC, 128], [0, 8],
                                                          [TC, 16], [1, TC]]),
                                             op=ALU.mult)
                        vector.drain()
                        if gi >= 2 and c % 16 == 0:
                            vector.wait_ge(s_wo, 8 * (gi - 1))
                        with nc.allow_low_precision(
                                reason="y feeds a bf16 matmul operand; the DVE "
                                       "reduce accumulates in fp32 internally"):
                            vector.tensor_reduce(
                                out=bass.AP(_t(y_st[gp]), (c % 16) * TC,
                                            [[DC * GRP, 128], [GRP, 8], [1, TC]]),
                                in_=bass.AP(_t(zB), 1,
                                            [[128 * LN, 128], [16 * LN, 8], [1, TC],
                                             [LN, 16]]),
                                axis=AX.X,
                                op=ALU.add).then_inc(s_y, 1)
                        vector.nop().then_inc(s_vec, 1)
                    if debug:
                        vector.wait_ge(s_sst[0], 16 * NG * DC // 2)
                        vector.wait_ge(s_sst[1], 16 * NG * DC // 2)

                if debug:
                    @blk.gpsimd
                    def _(g):
                        g.wait_ge(s_sst[0], 16 * NG * DC // 2)
                        g.wait_ge(s_sst[1], 16 * NG * DC // 2)
                        g.dma_start(out=dbg["s"],
                                    in_=bass.AP(_t(s_s), 0,
                                                [[T, DC * 128], [1, T]])
                                    ).then_inc(s_dbg, 16)
                        g.dma_start(out=dbg["u"], in_=u_s).then_inc(s_dbg, 16)
                        g.dma_start(out=dbg["xln"],
                                    in_=bass.AP(_t(xln_s), 0,
                                                [[T, DC * 128], [1, T]])
                                    ).then_inc(s_dbg, 16)
                        g.dma_start(out=dbg["y"],
                                    in_=y_st[1][:]).then_inc(s_dbg, 16)
                        g.wait_ge(s_dbg, 16 * 4)

        if phases < 5:
            return nc
        with ExitStack() as st3:
            NWS_SEM = 3
            sm3 = lambda n: st3.enter_context(nc.semaphore(n))
            f_xln = [sm3("f_xln%d" % q) for q in range(2)]
            f_gw = [sm3("f_gw%d" % q) for q in range(NWS_SEM)]
            f_uw = [sm3("f_uw%d" % q) for q in range(NWS_SEM)]
            f_xo = [sm3("f_xo%d" % q) for q in range(2)]
            f_dw, f_mm, f_cl = sm3("f_dw"), sm3("f_mm"), sm3("f_cl")
            f_si, f_g8, f_dn = sm3("f_si"), sm3("f_g8"), sm3("f_dn")
            f_ac = sm3("f_ac")
            f_stm = [sm3("f_stm%d" % q) for q in range(2)]
            f_gs, f_out = sm3("f_gs"), sm3("f_out")
            f_ost = [sm3("f_ost%d" % q) for q in range(4)]
            blk = st3.enter_context(nc.Block())
            sb3 = lambda n, s, d: st3.enter_context(nc.sbuf_tensor(n, s, d))
            xln_g = [sb3("xln_g%d" % i, [128, DC * GRP], BF16) for i in range(2)]
            NWS = 3
            NPS = 3
            gw = [sb3("gw%d" % i, [128, 1024], BF16) for i in range(NWS)]
            uw = [sb3("uw%d" % i, [128, 1024], BF16) for i in range(NWS)]
            wdn = sb3("wdn", [128, 32 * 1024], BF16)
            g8 = [sb3("g8_%d" % i, [128, 32 * GRP], BF16) for i in range(2)]
            acc = [sb3("acc%d" % i, [128, D], F32) for i in range(4)]
            clp = [sb3("clp%d" % i, [128, GRP], F32) for i in range(NPS)]
            sil = [sb3("sil%d" % i, [128, GRP], F32) for i in range(NPS)]
            gt = [sb3("gt%d" % i, [128, D], BF16) for i in range(2)]
            s_tm = [sb3("s_tm%d" % i, [128, D], BF16) for i in range(2)]
            xo = [sb3("xo%d" % i, [128, D], F32) for i in range(2)]
            ps_g = [st3.enter_context(nc.psum_tensor("ps_g%d" % i, [128, GRP], F32))
                    for i in range(NPS)]
            ps_u = [st3.enter_context(nc.psum_tensor("ps_u%d" % i, [128, GRP], F32))
                    for i in range(NPS)]
            ps_dn = st3.enter_context(nc.psum_tensor("ps_dn", [128, D], F32))

            def jj_of(gi, ei):
                return gi * JJ_PER_G + sum(EXPERTS[e][1] for e in range(ei))

            def dnslab_of(gi, ei):
                return gi * DN_PER_G + sum(EXPERTS[e][3] for e in range(ei))

            def comb_cnt(gi, ei):
                return gi * E + min(ei, E - 1) + 1

            @blk.sync
            def _(sync):
                wpend = []

                def wdn_block():
                    e9, dnb, ndc = wpend.pop(0)
                    if e9 >= 1:
                        sync.wait_ge(f_dn, 4 * e9)
                    for j in range(ndc):
                        sync.dma_start(out=wdn[:, j * 1024:(j + 1) * 1024],
                                       in_=dn_sp[dnb + j]).then_inc(f_dw, 16)

                for gi in range(NG):
                    if gi >= 2:
                        sync.wait_ge(f_mm, (gi - 1) * JJ_PER_G)
                    for cc in range(DC):
                        sync.dma_start(
                            out=xln_g[gi % 2][:, cc * GRP:(cc + 1) * GRP],
                            in_=bass.AP(_t(xln_s), cc * 128 * T + gi * GRP,
                                        [[T, 128], [1, GRP]])
                            ).then_inc(f_xln[gi % 2], 16)
                    for ei, (gub, nf, dnb, ndc, shared) in enumerate(EXPERTS):
                        jj0 = jj_of(gi, ei)
                        for j in range(nf):
                            jj = jj0 + j
                            if jj >= 3:
                                sync.wait_ge(f_mm, jj - 2)
                            sync.dma_start(out=gw[jj % 3][:],
                                           in_=gu_sp[gub + j]
                                           ).then_inc(f_gw[jj % 3], 16)
                            if j == 3 and wpend:
                                wdn_block()
                        wpend.append((gi * 9 + ei, dnb, ndc))
                while wpend:
                    wdn_block()

            @blk.gpsimd
            def _(g):
                def g_xo(gc, tb):
                    n = gc * 4 + tb
                    if n >= 2:
                        g.wait_ge(f_out, n - 1)
                    g.dma_start(out=xo[tb % 2][:],
                                in_=x_h[gc * GRP + tb * 128:
                                        gc * GRP + (tb + 1) * 128, :]
                                ).then_inc(f_xo[tb % 2], 16)

                def g_store(gc, tb):
                    g.wait_ge(f_out, gc * 4 + tb + 1)
                    g.dma_start(out=out_h[gc * GRP + tb * 128:
                                          gc * GRP + (tb + 1) * 128, :],
                                in_=acc[tb][:]).then_inc(f_ost[tb], 16)

                for gi in range(NG):
                    for ei, (gub, nf, dnb, ndc, shared) in enumerate(EXPERTS):
                        jj0 = jj_of(gi, ei)
                        xslots = ({2: 0, 4: 1, 13: 2, 18: 3}
                                  if (ei == 0 and gi >= 1) else {})
                        stslots = ({12: 0, 18: 1, 25: 2, 31: 3}
                                   if (ei == 0 and gi >= 1) else {})
                        for j in range(nf):
                            jj = jj0 + j
                            if jj >= 3:
                                g.wait_ge(f_mm, jj - 2)
                            g.dma_start(out=uw[jj % 3][:],
                                        in_=gu_sp[gub + nf + j]
                                        ).then_inc(f_uw[jj % 3], 16)
                            if j in xslots:
                                g_xo(gi - 1, xslots[j])
                            if j in stslots:
                                g_store(gi - 1, stslots[j])
                for tb in range(4):
                    g_xo(NG - 1, tb)
                    g_store(NG - 1, tb)
                for tbq in range(4):
                    g.wait_ge(f_ost[tbq], 16 * NG)

            @blk.tensor
            def _(tensor):
                pend = []

                def emit_down(tensor):
                    e9, ei, jj_end, dn_end, tb = pend.pop(0)
                    di = e9 * 4 + tb
                    tensor.wait_ge(f_g8, jj_end)
                    tensor.wait_ge(f_dw, 16 * dn_end)
                    if di >= 1:
                        tensor.wait_ge(f_ac, di)
                    for j in range(EXPERTS[ei][3]):
                        tensor.matmul(out=ps_dn[:, 0:512],
                                      lhsT=g8[e9 % 2][:, j * GRP + tb * 128:
                                                      j * GRP + (tb + 1) * 128],
                                      rhs=wdn[:, j * 1024:j * 1024 + 512],
                                      start=(j == 0), stop=(j == EXPERTS[ei][3] - 1))
                    for j in range(EXPERTS[ei][3]):
                        mm = tensor.matmul(
                            out=ps_dn[:, 512:1024],
                            lhsT=g8[e9 % 2][:, j * GRP + tb * 128:
                                            j * GRP + (tb + 1) * 128],
                            rhs=wdn[:, j * 1024 + 512:(j + 1) * 1024],
                            start=(j == 0), stop=(j == EXPERTS[ei][3] - 1))
                    mm.then_inc(f_dn, 1)

                for gi in range(NG):
                    gp = gi % 2
                    tensor.wait_ge(f_xln[gi % 2], 16 * DC * (gi // 2 + 1))
                    for ei, (gub, nf, dnb, ndc, shared) in enumerate(EXPERTS):
                        jj0 = jj_of(gi, ei)
                        slots = {min(nf - 1, max(1, (nf * (t + 1)) // 5) + 2): t
                                 for t in range(4)}
                        for j in range(nf):
                            jj = jj0 + j
                            pp = jj % NPS
                            tensor.wait_ge(f_gw[jj % 3], 16 * (jj // 3 + 1))
                            tensor.wait_ge(f_uw[jj % 3], 16 * (jj // 3 + 1))
                            if jj >= NPS:
                                tensor.wait_ge(f_g8, jj - NPS + 1)
                            for cc in range(DC):
                                tensor.matmul(out=ps_g[pp][:],
                                              lhsT=gw[pp][:, cc * 128:(cc + 1) * 128],
                                              rhs=xln_g[gp][:, cc * GRP:(cc + 1) * GRP],
                                              start=(cc == 0), stop=(cc == DC - 1))
                            for cc in range(DC):
                                mm = tensor.matmul(
                                    out=ps_u[pp][:],
                                    lhsT=uw[pp][:, cc * 128:(cc + 1) * 128],
                                    rhs=xln_g[gp][:, cc * GRP:(cc + 1) * GRP],
                                    start=(cc == 0), stop=(cc == DC - 1))
                            mm.then_inc(f_mm, 1)
                            if j in slots and pend:
                                emit_down(tensor)
                        e9 = gi * 9 + ei
                        while pend:
                            emit_down(tensor)
                        for tb in range(4):
                            pend.append((e9, ei, jj0 + nf,
                                         dnslab_of(gi, ei) + ndc, tb))
                while pend:
                    emit_down(tensor)

            @blk.vector
            def _(vector):
                def v_combine(gc, tb):
                    nn = gc * 4 + tb
                    vector.wait_ge(f_gs, nn + 1)
                    vector.wait_ge(f_xo[tb % 2], 16 * (nn // 2 + 1))
                    vector.wait_ge(f_stm[tb % 2], 16 * 8 * (nn // 2 + 1))
                    vector.tensor_mul(out=acc[tb][:], in0=s_tm[tb % 2][:],
                                      in1=gt[tb % 2][:])
                    vector.drain()
                    vector.tensor_add(out=acc[tb][:], in0=acc[tb][:],
                                      in1=xo[tb % 2][:]).then_inc(f_out, 1)

                vpend = []

                def v_evac(di, first):
                    e9, tb = di // 4, di % 4
                    gi, ei = e9 // 9, e9 % 9
                    cb = (comb_sb[:, (gi * 4 + tb) * 8 + ei:
                                  (gi * 4 + tb) * 8 + ei + 1]
                          if ei < E else None)
                    if tb == 0 and not first:
                        vector.drain()
                    vector.wait_ge(f_dn, di + 1)
                    if first:
                        gacc = di // 36
                        if gacc >= 1:
                            vector.wait_ge(f_ost[tb], 16 * gacc)
                        vector.tensor_scalar_mul(out=acc[tb][:], in0=ps_dn[:],
                                                 scalar1=cb).then_inc(f_ac, 1)
                    elif ei < E:
                        vector.scalar_tensor_tensor(out=acc[tb][:], in0=ps_dn[:],
                                                    scalar=cb, in1=acc[tb][:],
                                                    op0=ALU.mult,
                                                    op1=ALU.add).then_inc(f_ac, 1)
                    else:
                        vector.tensor_add(out=acc[tb][:], in0=acc[tb][:],
                                          in1=ps_dn[:]).then_inc(f_ac, 1)

                for gi in range(NG):
                    for ei, (gub, nf, dnb, ndc, shared) in enumerate(EXPERTS):
                        e9 = gi * 9 + ei
                        jj0 = jj_of(gi, ei)
                        cslots = ({12: 0, 18: 1, 25: 2, 31: 3}
                                  if (ei == 0 and gi >= 1) else {})
                        eslots = {min(nf - 1, max(2, (nf * (t + 1)) // 5) + 3): t
                                  for t in range(4)}
                        for j in range(nf):
                            jj = jj0 + j
                            pp = jj % NPS
                            vector.wait_ge(f_mm, jj + 1)
                            vector.tensor_scalar(out=clp[pp][:], in0=ps_g[pp][:],
                                                 scalar1=10.0, scalar2=-10.0,
                                                 op0=ALU.min,
                                                 op1=ALU.max).then_inc(f_cl, 1)
                            vector.wait_ge(f_si, jj + 1)
                            if j == 0 and e9 >= 2:
                                vector.wait_ge(f_dn, 4 * (e9 - 1))
                            vector.tensor_mul(out=g8[e9 % 2][:, j * GRP:(j + 1) * GRP],
                                              in0=sil[pp][:],
                                              in1=ps_u[pp][:]).then_inc(f_g8, 1)
                            if j in cslots:
                                v_combine(gi - 1, cslots[j])
                            if j in eslots and vpend:
                                v_evac(*vpend.pop(0))
                        while vpend:
                            v_evac(*vpend.pop(0))
                        for tb in range(4):
                            vpend.append((e9 * 4 + tb, ei == 0))
                while vpend:
                    v_evac(*vpend.pop(0))
                for tb in range(4):
                    v_combine(NG - 1, tb)
                for tbq in range(4):
                    vector.wait_ge(f_ost[tbq], 16 * NG)

            @blk.scalar
            def _(scalar):
                def s_combine(gc, tb):
                    nn = gc * 4 + tb
                    if nn >= 2:
                        scalar.wait_ge(f_out, nn - 1)
                    for oc in range(DC):
                        scalar.dma_start_transpose(
                            out=s_tm[tb % 2][:, oc * 128:(oc + 1) * 128],
                            in_=bass.AP(_t(s_s),
                                        oc * 128 * T + gc * GRP + tb * 128,
                                        [[T, 128], [1, 128]])
                        ).then_inc(f_stm[tb % 2], 16)
                    scalar.wait_ge(f_ac, gc * 36 + 32 + tb + 1)
                    scalar.activation(out=gt[tb % 2][:], in_=acc[tb][:],
                                      func=AF.Silu).then_inc(f_gs, 1)

                for gi in range(NG):
                    for ei, (gub, nf, dnb, ndc, shared) in enumerate(EXPERTS):
                        jj0 = jj_of(gi, ei)
                        sslots = ({10: 0, 16: 1, 23: 2, 29: 3}
                                  if (ei == 0 and gi >= 1) else {})
                        for j in range(nf):
                            jj = jj0 + j
                            pp = jj % NPS
                            scalar.wait_ge(f_cl, jj + 1)
                            if jj >= NPS:
                                scalar.wait_ge(f_g8, jj - NPS + 1)
                            scalar.activation(out=sil[pp][:], in_=clp[pp][:],
                                              func=AF.Silu).then_inc(f_si, 1)
                            if j in sslots:
                                s_combine(gi - 1, sslots[j])
                for tb in range(4):
                    s_combine(NG - 1, tb)

    return nc


def _prep_inputs(ins):
    f32 = np.float32
    c = np.ascontiguousarray
    A_log = np.asarray(ins["L0_attn_A_log"], f32)
    W_dt = np.asarray(ins["L0_attn_W_dt"], f32)
    b_dt = np.asarray(ins["L0_attn_b_dt"], f32)
    W_xp = np.asarray(ins["L0_attn_W_xp"], f32)
    b_xp = np.asarray(ins["L0_attn_b_xp"], f32)
    W_out = np.asarray(ins["L0_attn_W_out"], f32)
    b_out = np.asarray(ins["L0_attn_b_out"], f32)
    gam = np.asarray(ins["L0_ffn_ln_gamma"], f32)
    bet = np.asarray(ins["L0_ffn_ln_beta"], f32)
    rw = np.asarray(ins["L0_ffn_router_w"], f32)
    gu = np.asarray(ins["L0_ffn_gate_up_w"], f32)
    dn = np.asarray(ins["L0_ffn_down_w"], f32)
    sgu = np.asarray(ins["L0_ffn_shared_gate_up_w"], f32)
    sdn = np.asarray(ins["L0_ffn_shared_down_w"], f32)

    m = {}
    m["a_lane"] = c(A_log.reshape(8, 128, 16).transpose(1, 0, 2).reshape(128, 128))
    m["wdt_fm"] = c(W_dt[:, 0].reshape(8, 128).T)
    m["bdt_fm"] = c(b_dt.reshape(8, 128).T)
    m["wxpt_fm"] = c(W_xp.T.reshape(8, 128, 32).transpose(1, 0, 2).reshape(128, 256))
    m["bxp_col"] = c(b_xp.reshape(32, 1))
    m["woutt_fm"] = c(W_out.T.reshape(8, 128, 1024).transpose(1, 0, 2)
                      .reshape(128, 8192))
    m["bout_fm"] = c(b_out.reshape(8, 128).T)
    m["gam_rep"] = c(np.broadcast_to(gam, (128, D)))
    m["bet_rep"] = c(np.broadcast_to(bet, (128, D)))
    m["rw_fm"] = c(rw.reshape(8, 128, 8).transpose(1, 0, 2).reshape(128, 64))
    gut = gu.reshape(E, 8, 128, 64, 128).transpose(0, 3, 2, 1, 4) \
            .reshape(E * 64, 128, 1024)
    sgut = sgu.reshape(8, 128, 16, 128).transpose(2, 1, 0, 3).reshape(16, 128, 1024)
    m["gu_tiled"] = c(np.concatenate([gut, sgut], axis=0))
    dnt = dn.reshape(E * 32, 128, 1024)
    sdnt = sdn.reshape(8, 128, 1024)
    m["dn_tiled"] = c(np.concatenate([dnt, sdnt], axis=0))
    m["ident_b"] = np.eye(128, dtype=BF)
    m["ident_f"] = np.eye(128, dtype=np.float32)
    m["ones_f"] = np.ones((1, 128), dtype=np.float32)
    return m


def kernel(**inputs):
    x = np.asarray(inputs["L0_attn_x"], np.float32)
    base = _prep_inputs(inputs)
    nc = build_nc(debug=False)
    in_maps = []
    for core in range(8):
        im = dict(base)
        im["x"] = np.ascontiguousarray(x[core])
        in_maps.append(im)
    res = run_bass_kernel_spmd(nc, in_maps, core_ids=list(range(8)))
    return np.stack([np.asarray(r["out"]) for r in res.results], axis=0)


# version 12 (final)
# speedup vs baseline: 1.0373x; 1.0373x over previous
import numpy as np
import ml_dtypes
from contextlib import ExitStack

import concourse.bass as bass
import concourse.mybir as mybir
from concourse.bass_utils import run_bass_kernel_spmd

F32 = mybir.dt.float32
BF16 = mybir.dt.bfloat16
AF = mybir.ActivationFunctionType
ALU = mybir.AluOpType
AX = mybir.AxisListType
BF = ml_dtypes.bfloat16

D = 1024
T = 4096
NS = 16
E = 8
DC = 8
NT = 32
TC = 32
LN = TC + 1
NCH = T // TC
GRP = 512
NG = T // GRP
EPS = 1e-5

N_GU = E * 64 + 16
N_DN = E * 32 + 8

EXPERTS = [(e * 64, 32, e * 32, 32, False) for e in range(E)] + \
          [(E * 64, 8, E * 32, 8, True)]
JJ_PER_G = sum(nf for _, nf, _, _, _ in EXPERTS)
DN_PER_G = sum(nd for _, _, _, nd, _ in EXPERTS)



def _t(x):
    return x.tensor if isinstance(x, bass.AP) else x

def build_nc(debug=False, phases=5):
    nc = bass.Bass("TRN2", target_bir_lowering=False)

    x_h = nc.dram_tensor("x", [T, D], F32, kind="ExternalInput").ap()
    a_lane_h = nc.dram_tensor("a_lane", [128, 128], F32, kind="ExternalInput").ap()
    wdt_h = nc.dram_tensor("wdt_fm", [128, 8], F32, kind="ExternalInput").ap()
    bdt_h = nc.dram_tensor("bdt_fm", [128, 8], F32, kind="ExternalInput").ap()
    wxpt_h = nc.dram_tensor("wxpt_fm", [128, 256], F32, kind="ExternalInput").ap()
    bxp_h = nc.dram_tensor("bxp_col", [32, 1], F32, kind="ExternalInput").ap()
    woutt_h = nc.dram_tensor("woutt_fm", [128, 8192], F32, kind="ExternalInput").ap()
    bout_h = nc.dram_tensor("bout_fm", [128, 8], F32, kind="ExternalInput").ap()
    gam_h = nc.dram_tensor("gam_rep", [128, D], F32, kind="ExternalInput").ap()
    bet_h = nc.dram_tensor("bet_rep", [128, D], F32, kind="ExternalInput").ap()
    rw_h = nc.dram_tensor("rw_fm", [128, 64], F32, kind="ExternalInput").ap()
    gu_h = nc.dram_tensor("gu_tiled", [N_GU, 128, 1024], F32, kind="ExternalInput").ap()
    dn_h = nc.dram_tensor("dn_tiled", [N_DN, 128, 1024], F32, kind="ExternalInput").ap()
    idb_h = nc.dram_tensor("ident_b", [128, 128], BF16, kind="ExternalInput").ap()
    idf_h = nc.dram_tensor("ident_f", [128, 128], F32, kind="ExternalInput").ap()
    ones_h = nc.dram_tensor("ones_f", [1, 128], F32, kind="ExternalInput").ap()

    out_h = nc.dram_tensor("out", [T, D], F32, kind="ExternalOutput").ap()

    xln_s = nc.dram_tensor("xln_s", [DC, 128, T], BF16, kind="Internal").ap()
    xn_sc = nc.dram_tensor("xn_sc", [NCH, 128, 8 * TC], BF16, kind="Internal").ap()
    bc_s = nc.dram_tensor("bc_s", [32, T], F32, kind="Internal").ap()
    u_s = nc.dram_tensor("u_s", [1, T], F32, kind="Internal").ap()
    gu_sp = nc.dram_tensor("gu_sp", [N_GU, 128, 1024], BF16, kind="Internal").ap()
    dn_sp = nc.dram_tensor("dn_sp", [N_DN, 128, 1024], BF16, kind="Internal").ap()
    s_s = nc.dram_tensor("s_s", [DC, 128, T], BF16, kind="Internal").ap()

    dbg = {}
    if debug:
        dbg["xn"] = nc.dram_tensor("dbg_xn", [128, DC * T], BF16, kind="ExternalOutput").ap()
        dbg["bc"] = nc.dram_tensor("dbg_bc", [32, T], F32, kind="ExternalOutput").ap()
        dbg["comb"] = nc.dram_tensor("dbg_comb", [E, T], F32, kind="ExternalOutput").ap()
        dbg["s"] = nc.dram_tensor("dbg_s", [DC * 128, T], BF16, kind="ExternalOutput").ap()
        dbg["u"] = nc.dram_tensor("dbg_u", [1, T], F32, kind="ExternalOutput").ap()
        dbg["xln"] = nc.dram_tensor("dbg_xln", [DC * 128, T], BF16, kind="ExternalOutput").ap()
        dbg["y"] = nc.dram_tensor("dbg_y", [128, DC * GRP], BF16, kind="ExternalOutput").ap()

    with ExitStack() as stk:
        sbA = lambda n, s, d: stk.enter_context(nc.sbuf_tensor(n, s, d))
        a_lane = sbA("a_lane_sb", [128, 128], F32)
        wdt = sbA("wdt_sb", [128, 8], F32)
        bdt = sbA("bdt_sb", [128, 8], F32)
        bxp = sbA("bxp_sb", [32, 1], F32)
        bout = sbA("bout_sb", [128, 8], F32)
        idb = sbA("idb_sb", [128, 128], BF16)
        idf = sbA("idf_sb", [128, 128], F32)
        ones_f = sbA("ones_sb", [1, 128], F32)
        c_eps = sbA("c_eps", [128, 1], F32)
        c_one = sbA("c_one", [128, 1], F32)
        c_ten = sbA("c_ten", [128, 1], F32)
        c_nten = sbA("c_nten", [128, 1], F32)
        comb_sb = sbA("comb_sb", [128, NT * 8], F32)

        if phases < 1:
            return nc
        with nc.semaphore("c_lds") as c_lds, nc.Block() as blk:
            @blk.sync
            def _(sync):
                for dst, src in [(a_lane[:], a_lane_h), (wdt[:], wdt_h),
                                 (bdt[:], bdt_h), (bxp[:], bxp_h),
                                 (bout[:], bout_h), (idb[:], idb_h),
                                 (idf[:], idf_h), (ones_f[:], ones_h)]:
                    sync.dma_start(out=dst, in_=src).then_inc(c_lds, 16)

            @blk.vector
            def _(vector):
                vector.memset(c_eps[:], EPS)
                vector.memset(c_one[:], 1.0)
                vector.memset(c_ten[:], 10.0)
                vector.memset(c_nten[:], -10.0)
                vector.drain()

            @blk.scalar
            def _(scalar):
                scalar.wait_ge(c_lds, 16 * 8)
                scalar.activation(out=a_lane[:], in_=a_lane[:], func=AF.Exp)
                scalar.drain()
                scalar.mul(out=a_lane[:], in_=a_lane[:], mul=-1.0)
                scalar.drain()

        with ExitStack() as stkA:
            sbB = lambda n, s, d: stkA.enter_context(nc.sbuf_tensor(n, s, d))
            woutt_b = sbB("woutt_b", [128, 8192], BF16)

            if phases < 2:
                return nc
            with ExitStack() as st0:
                sm0 = lambda n: st0.enter_context(nc.semaphore(n))
                p_w, p_x0, p_x1 = sm0("p_w"), sm0("p_x0"), sm0("p_x1")
                p_sq, p_va, p_sc = sm0("p_sq"), sm0("p_va"), sm0("p_sc")
                p_vb, p_sc2, p_rdy = sm0("p_vb"), sm0("p_sc2"), sm0("p_rdy")
                p_tp, p_ev, p_mm = sm0("p_tp"), sm0("p_ev"), sm0("p_mm")
                p_sm, p_q = sm0("p_sm"), sm0("p_q")
                p_cbS = sm0("p_cbS")
                p_stu, p_stx, p_stf = sm0("p_stu"), sm0("p_stx"), sm0("p_stf")
                p_stn = sm0("p_stn")
                blk = st0.enter_context(nc.Block())
                sb0 = lambda n, s, d: st0.enter_context(nc.sbuf_tensor(n, s, d))
                xb = [sb0("xb%d" % i, [128, D], F32) for i in range(2)]
                gam = sb0("gam", [128, D], F32)
                bet = sb0("bet", [128, D], F32)
                rw_f = sb0("rw_f", [128, 64], F32)
                wxpt_f = sb0("wxpt_f", [128, 256], F32)
                rw_b = sb0("rw_b", [128, 64], BF16)
                wxpt_b = sb0("wxpt_b", [128, 256], BF16)
                sx = sb0("sx", [128, 1], F32)
                sx2 = sb0("sx2", [128, 1], F32)
                mu = sb0("mu", [128, 1], F32)
                var = sb0("var", [128, 1], F32)
                vtmp = sb0("vtmp", [128, 1], F32)
                vtmp2 = sb0("vtmp2", [128, 1], F32)
                rstd = sb0("rstd", [128, 1], F32)
                rstd2 = sb0("rstd2", [128, 1], F32)
                xn_f = sb0("xn_f", [128, D], F32)
                xn_t = sb0("xn_t", [128, D], BF16)
                xln_a = sb0("xln_a", [128, D], F32)
                xln_t = sb0("xln_t", [128, D], BF16)
                xlf = sb0("xlf", [128, DC * 128], BF16)
                xnf_st = sb0("xnf_st", [128, DC * 128], BF16)
                u_col = sb0("u_col", [128, 1], F32)
                sq_t = sb0("sq_t", [128, D], F32)
                q_t = sb0("q_t", [128, 8], F32)
                m1 = sb0("m1", [128, 1], F32)
                m1n = sb0("m1n", [128, 1], F32)
                qm = sb0("qm", [128, 8], F32)
                m2 = sb0("m2", [128, 1], F32)
                den = sb0("den", [128, 1], F32)
                rden = sb0("rden", [128, 1], F32)
                mask = sb0("mask", [128, 8], F32)
                bc_fm = sb0("bc_fm", [32, T], F32)
                ps_t = [st0.enter_context(nc.psum_tensor("ps_t%d" % i, [128, 128], BF16))
                        for i in range(4)]
                ps_r = st0.enter_context(nc.psum_tensor("ps_r", [128, 8], F32))
                ps_bc = st0.enter_context(nc.psum_tensor("ps_bc", [32, 128], F32))

                p_x = [p_x0, p_x1]

                @blk.sync
                def _(sync):
                    for dst, src in [(gam[:], gam_h), (bet[:], bet_h),
                                     (rw_f[:], rw_h), (wxpt_f[:], wxpt_h)]:
                        sync.dma_start(out=dst, in_=src).then_inc(p_w, 16)
                    for k in range(NT):
                        if k >= 2:
                            sync.wait_ge(p_rdy, k - 1)
                        sync.dma_start(out=xb[k % 2][:],
                                       in_=x_h[k * 128:(k + 1) * 128, :]
                                       ).then_inc(p_x[k % 2], 16)

                @blk.scalar
                def _(scalar):
                    scalar.wait_ge(p_w, 16 * 4)
                    scalar.copy(out=rw_b[:], in_=rw_f[:])
                    scalar.copy(out=wxpt_b[:], in_=wxpt_f[:])
                    scalar.drain()
                    scalar.nop().then_inc(p_sq, 1)
                    for k in range(NT):
                        xk = xb[k % 2]
                        scalar.wait_ge(p_x[k % 2], 16 * (k // 2 + 1))
                        scalar.activation(out=sq_t[:], in_=xk[:], func=AF.Square,
                                          accum_out=sx2[:]).then_inc(p_sq, 1)
                        scalar.wait_ge(p_va, k + 1)
                        scalar.activation(out=vtmp[:], in_=var[:], func=AF.Ln,
                                          bias=c_eps[:])
                        scalar.drain()
                        scalar.activation(out=rstd[:], in_=vtmp[:], func=AF.Exp,
                                          scale=-0.5).then_inc(p_sc, 1)
                        scalar.wait_ge(p_vb, k + 1)
                        scalar.activation(out=vtmp2[:], in_=vtmp2[:], func=AF.Ln,
                                          bias=c_eps[:])
                        scalar.drain()
                        scalar.activation(out=rstd2[:], in_=vtmp2[:], func=AF.Exp,
                                          scale=-0.5).then_inc(p_sc2, 1)
                        scalar.wait_ge(p_sm, 2 * k + 1)
                        scalar.activation(out=q_t[:], in_=ps_r[:], func=AF.Exp,
                                          bias=m1n[:]).then_inc(p_q, 1)
                        scalar.wait_ge(p_mm, 2 * k + 2)
                        scalar.activation(out=bc_fm[:, k * 128:(k + 1) * 128],
                                          in_=ps_bc[:], func=AF.Identity,
                                          bias=bxp[:]).then_inc(p_cbS, 1)

                @blk.vector
                def _(vector):
                    for k in range(NT):
                        xk = xb[k % 2]
                        vector.wait_ge(p_x[k % 2], 16 * (k // 2 + 1))
                        vector.wait_ge(p_sq, k + 2)
                        vector.tensor_reduce(out=sx[:], in_=xk[:], axis=AX.X,
                                             op=ALU.add)
                        vector.drain()
                        vector.tensor_scalar_mul(out=mu[:], in0=sx[:],
                                                 scalar1=1.0 / D)
                        vector.drain()
                        vector.tensor_mul(out=vtmp[:], in0=mu[:], in1=mu[:])
                        vector.tensor_scalar_mul(out=var[:], in0=sx2[:],
                                                 scalar1=1.0 / D)
                        vector.drain()
                        vector.tensor_sub(out=var[:], in0=var[:],
                                          in1=vtmp[:]).then_inc(p_va, 1)
                        vector.wait_ge(p_sc, k + 1)
                        vector.tensor_mul(out=vtmp2[:], in0=var[:], in1=rstd[:])
                        vector.drain()
                        vector.tensor_mul(out=vtmp2[:], in0=vtmp2[:],
                                          in1=rstd[:]).then_inc(p_vb, 1)
                        if k >= 1:
                            vector.wait_ge(p_tp, 16 * k)
                            vector.wait_ge(p_stu, 16 * k)
                        vector.tensor_scalar(out=xn_f[:], in0=xk[:], scalar1=mu[:],
                                             scalar2=rstd[:], op0=ALU.subtract,
                                             op1=ALU.mult)
                        vector.drain()
                        vector.tensor_copy(out=xn_t[:], in_=xn_f[:])
                        vector.tensor_copy(out=u_col[:], in_=xn_f[:, 0:1])
                        vector.wait_ge(p_sc2, k + 1)
                        vector.scalar_tensor_tensor(out=xln_a[:], in0=xn_f[:],
                                                    scalar=rstd2[:], in1=gam[:],
                                                    op0=ALU.mult, op1=ALU.mult)
                        vector.drain()
                        vector.tensor_add(out=xln_t[:], in0=xln_a[:],
                                          in1=bet[:]).then_inc(p_rdy, 1)
                        if k >= 1:
                            vector.wait_ge(p_stx, 16 * 8 * k)
                            vector.wait_ge(p_stn, 16 * 8 * k)
                        for c in range(DC):
                            for which in range(2):
                                n = 16 * k + 2 * c + which + 1
                                vector.wait_ge(p_tp, n)
                                pst = ps_t[(2 * c + which) % 4]
                                if which == 0:
                                    vector.tensor_copy(
                                        out=xnf_st[:, c * 128:(c + 1) * 128],
                                        in_=pst[:]).then_inc(p_ev, 1)
                                else:
                                    vector.tensor_copy(
                                        out=xlf[:, c * 128:(c + 1) * 128],
                                        in_=pst[:]).then_inc(p_ev, 1)
                        vector.wait_ge(p_mm, 2 * k + 1)
                        vector.tensor_reduce(out=m1[:], in_=ps_r[:], axis=AX.X,
                                             op=ALU.max)
                        vector.drain()
                        vector.tensor_scalar_mul(out=m1n[:], in0=m1[:],
                                                 scalar1=-1.0).then_inc(p_sm, 1)
                        vector.wait_ge(p_q, k + 1)
                        vector.tensor_scalar(out=mask[:], in0=q_t[:],
                                             scalar1=0.999999, scalar2=None,
                                             op0=ALU.is_ge)
                        vector.drain()
                        vector.scalar_tensor_tensor(out=qm[:], in0=mask[:],
                                                    scalar=-2.0, in1=q_t[:],
                                                    op0=ALU.mult, op1=ALU.add)
                        vector.drain()
                        vector.tensor_reduce(out=m2[:], in_=qm[:], axis=AX.X,
                                             op=ALU.max)
                        vector.drain()
                        vector.tensor_scalar_add(out=den[:], in0=m2[:],
                                                 scalar1=1.0)
                        vector.drain()
                        vector.reciprocal(out=rden[:], in_=den[:])
                        vector.tensor_scalar(out=mask[:], in0=q_t[:],
                                             scalar1=m2[:], scalar2=None,
                                             op0=ALU.is_ge)
                        vector.drain()
                        cslice = comb_sb[:, k * 8:(k + 1) * 8]
                        vector.tensor_mul(out=cslice, in0=q_t[:], in1=mask[:])
                        vector.drain()
                        vector.tensor_scalar_mul(out=cslice, in0=cslice,
                                                 scalar1=rden[:]).then_inc(p_sm, 1)
                    if debug:
                        vector.drain()

                @blk.tensor
                def _(tensor):
                    tensor.wait_ge(p_sq, 1)
                    for k in range(NT):
                        tensor.wait_ge(p_rdy, k + 1)
                        for c in range(DC):
                            for which in range(2):
                                n = 16 * k + 2 * c + which + 1
                                if n > 4:
                                    tensor.wait_ge(p_ev, n - 4)
                                pst = ps_t[(2 * c + which) % 4]
                                src = xn_t if which == 0 else xln_t
                                tensor.transpose(out=pst[:],
                                                 in_=src[:, c * 128:(c + 1) * 128],
                                                 identity=idb[:]).then_inc(p_tp, 1)
                        tensor.wait_ge(p_ev, 16 * k + 16)
                        if k >= 1:
                            tensor.wait_ge(p_q, k)
                        for c in range(DC):
                            mm = tensor.matmul(out=ps_r[:],
                                               lhsT=xlf[:, c * 128:(c + 1) * 128],
                                               rhs=rw_b[:, c * 8:(c + 1) * 8],
                                               start=(c == 0), stop=(c == DC - 1))
                        mm.then_inc(p_mm, 1)
                        if k >= 1:
                            tensor.wait_ge(p_cbS, k)
                        for c in range(DC):
                            mm = tensor.matmul(out=ps_bc[:],
                                               lhsT=wxpt_b[:, c * 32:(c + 1) * 32],
                                               rhs=xnf_st[:, c * 128:(c + 1) * 128],
                                               start=(c == 0), stop=(c == DC - 1))
                        mm.then_inc(p_mm, 1)

                @blk.gpsimd
                def _(g):
                    for k in range(NT):
                        g.wait_ge(p_rdy, k + 1)
                        g.dma_start(out=u_s[0:1, k * 128:(k + 1) * 128],
                                    in_=u_col[:]).then_inc(p_stu, 16)
                        g.wait_ge(p_ev, 16 * k + 16)
                        for c in range(DC):
                            g.dma_start(out=bass.AP(_t(xln_s),
                                                    c * 128 * T + k * 128,
                                                    [[T, 128], [1, 128]]),
                                        in_=xlf[:, c * 128:(c + 1) * 128]
                                        ).then_inc(p_stx, 16)
                        for c in range(DC):
                            g.dma_start(
                                out=bass.AP(_t(xn_sc),
                                            (4 * k) * (128 * 8 * TC) + c * TC,
                                            [[8 * TC, 128], [128 * 8 * TC, 4],
                                             [1, TC]]),
                                in_=xnf_st[:, c * 128:(c + 1) * 128]
                            ).then_inc(p_stn, 16)
                    g.wait_ge(p_cbS, NT)
                    g.dma_start(out=bc_s, in_=bc_fm[:]).then_inc(p_stf, 16)
                    nst = 1
                    if debug:
                        g.dma_start(out=dbg["bc"], in_=bc_fm[:]).then_inc(p_stf, 16)
                        g.dma_start(out=dbg["comb"], in_=comb_sb[:]).then_inc(p_stf, 16)
                        g.dma_start(out=dbg["xn"],
                                    in_=bass.AP(_t(xn_sc), 0,
                                                [[8 * TC, 128],
                                                 [128 * 8 * TC, NCH], [1, 8 * TC]])
                                    ).then_inc(p_stf, 16)
                        nst += 3
                    g.wait_ge(p_stu, 16 * NT)
                    g.wait_ge(p_stx, 16 * 8 * NT)
                    g.wait_ge(p_stn, 16 * 8 * NT)
                    g.wait_ge(p_stf, 16 * nst)

            if phases < 4:
                return nc
            with ExitStack() as st2:
                sm2 = lambda n: st2.enter_context(nc.semaphore(n))
                s_stg = [sm2("s_stg%d" % q) for q in range(2)]
                s_ur = [sm2("s_ur%d" % q) for q in range(4)]
                s_sst = [sm2("s_sst%d" % q) for q in range(DC)]
                s_bc, s_dt1, s_sp = sm2("s_bc"), sm2("s_dt1"), sm2("s_sp")
                s_dtc, s_dt2, s_da = sm2("s_dtc"), sm2("s_dt2"), sm2("s_da")
                s_vec, s_y, s_wo = sm2("s_vec"), sm2("s_y"), sm2("s_wo")
                s_sev = sm2("s_sev")
                s_xn = [sm2("s_xn%d" % q) for q in range(2)]
                s_zc = sm2("s_zc")
                w_in = [sm2("w_in%d" % q) for q in range(8)]
                w_sp = [sm2("w_sp%d" % q) for q in range(8)]
                w_cse = sm2("w_cse")
                slabs = [(2, i) for i in range(8)] + \
                        [(0, i) for i in range(N_GU)] + \
                        [(1, i) for i in range(N_DN)]
                NSL = len(slabs)
                slab_of_chunk = [
                    list(range((NSL * c) // NCH, (NSL * (c + 1)) // NCH))
                    for c in range(NCH)]
                spill_ord = {}
                _spq = [0] * 8
                for _i, (_k, _x) in enumerate(slabs):
                    if _k != 2:
                        _spq[_i % 8] += 1
                        spill_ord[_i] = _spq[_i % 8]
                s_dbg = sm2("s_dbg")
                blk = st2.enter_context(nc.Block())
                sb2 = lambda n, s, d: st2.enter_context(nc.sbuf_tensor(n, s, d))
                fb = [sb2("fb%d" % i, [128, 1024], F32) for i in range(8)]
                bb = [sb2("bb%d" % i, [128, 1024], BF16) for i in range(8)]
                dAarg = sb2("dAarg", [128, 128 * TC], F32)
                dAb = [sb2("dAb%d" % i, [128, 128 * LN], F32) for i in range(2)]
                zB = sb2("zB", [128, 128 * LN], F32)
                hB = sb2("hB", [128, 128 * LN], F32)
                urep = [sb2("urep%d" % i, [128, TC], F32) for i in range(4)]
                dtp = [sb2("dtp%d" % i, [128, 8 * TC], F32) for i in range(2)]
                sp_a = [sb2("sp_a%d" % i, [128, 8 * TC], F32) for i in range(2)]
                dtc = [sb2("dtc%d" % i, [128, 8 * TC], F32) for i in range(2)]
                xnc = [sb2("xnc%d" % i, [128, 8 * TC], BF16) for i in range(2)]
                zc = [sb2("zc%d" % i, [128, 8 * TC], F32) for i in range(2)]
                brow = [sb2("brow%d" % i, [1, NS * TC], F32) for i in range(2)]
                crow = [sb2("crow%d" % i, [1, NS * TC], F32) for i in range(2)]
                y_st = [sb2("y_st%d" % i, [128, DC * GRP], BF16) for i in range(2)]
                s_tile = [sb2("s_tile%d" % i, [128, GRP], BF16) for i in range(DC)]
                ps_b = [st2.enter_context(nc.psum_tensor("ps_b%d" % i, [128, NS * TC], F32))
                        for i in range(2)]
                ps_c = [st2.enter_context(nc.psum_tensor("ps_c%d" % i, [128, NS * TC], F32))
                        for i in range(2)]
                ps_wo = [st2.enter_context(nc.psum_tensor("ps_wo%d" % i, [128, GRP], F32))
                         for i in range(2)]

                def lane_ap(t, off):
                    return bass.AP(_t(t), off,
                                   [[128 * LN, 128], [16 * LN, 8], [LN, 16], [1, TC]])

                def slot0_ap(t, off=0):
                    return bass.AP(_t(t), off, [[128 * LN, 128], [LN, 128], [1, 1]])

                def flat_ap(t):
                    return bass.AP(_t(t), 0, [[128 * LN, 128], [1, 128 * LN]])

                def cmp_ap(t, nrep=False):
                    return bass.AP(_t(t), 0,
                                   [[8 * TC, 128], [TC, 8], [0, 16], [1, TC]])

                @blk.sync
                def _(sync):
                    for c in range(NCH):
                        pp = c % 2
                        if c >= 2:
                            sync.wait_ge(s_bc, c - 1)
                        sync.dma_start(out=brow[pp][:],
                                       in_=bass.AP(_t(bc_s), c * TC,
                                                   [[NS * TC, 1], [T, NS], [1, TC]])
                                       ).then_inc(s_stg[pp], 16)
                        sync.dma_start(out=crow[pp][:],
                                       in_=bass.AP(_t(bc_s), NS * T + c * TC,
                                                   [[NS * TC, 1], [T, NS], [1, TC]])
                                       ).then_inc(s_stg[pp], 16)
                        for cu in ([0, 1] if c == 0 else []) + ([c + 2] if c + 2 < NCH else []):
                            if cu >= 4:
                                sync.wait_ge(s_dt1, cu - 1)
                            sync.dma_start(out=urep[cu % 4][:],
                                           in_=u_s[0:1, cu * TC:(cu + 1) * TC]
                                           .partition_broadcast(128)
                                           ).then_inc(s_ur[cu % 4], 16)
                        if c >= 2:
                            sync.wait_ge(s_zc, c - 1)
                        sync.dma_start(out=xnc[pp][:],
                                       in_=xn_sc[c]).then_inc(s_xn[pp], 16)
                        if c >= 2 and (c - 2) % 16 == 15:
                            gi = (c - 2) // 16
                            for oc in range(DC):
                                n = gi * 8 + oc
                                sync.wait_ge(s_sev, n + 1)
                                sync.dma_start(
                                    out=bass.AP(_t(s_s), oc * 128 * T + gi * GRP,
                                                [[T, 128], [1, GRP]]),
                                    in_=s_tile[oc][:]).then_inc(s_sst[oc], 16)
                    for oc in range(DC):
                        n = (NG - 1) * 8 + oc
                        sync.wait_ge(s_sev, n + 1)
                        sync.dma_start(out=bass.AP(_t(s_s),
                                                   oc * 128 * T + (NG - 1) * GRP,
                                                   [[T, 128], [1, GRP]]),
                                       in_=s_tile[oc][:]).then_inc(s_sst[oc], 16)
                    for oc in range(DC):
                        sync.wait_ge(s_sst[oc], 16 * NG)

                @blk.tensor
                def _(tensor):
                    for c in range(NCH):
                        pp = c % 2
                        tensor.wait_ge(s_stg[pp], 32 * (c // 2 + 1))
                        if c >= 2:
                            tensor.wait_ge(s_vec, c - 1)
                        tensor.matmul(out=ps_b[pp][:], lhsT=ones_f[:],
                                      rhs=brow[pp][:], start=True, stop=True)
                        tensor.matmul(out=ps_c[pp][:], lhsT=ones_f[:],
                                      rhs=crow[pp][:], start=True,
                                      stop=True).then_inc(s_bc, 1)
                        for gi in ([c // 16 - 1] if (c % 16 == 1 and c > 16)
                                   else ([NG - 1] if c == NCH - 1 else [])):
                            gp = gi % 2
                            if gi == 0:
                                tensor.wait_ge(w_cse, 8)
                            tensor.wait_ge(s_y, (gi + 1) * 16)
                            for oc in range(DC):
                                op = oc % 2
                                n = gi * 8 + oc
                                if n >= 2:
                                    tensor.wait_ge(s_sev, n - 1)
                                for dc in range(DC):
                                    mm = tensor.matmul(
                                        out=ps_wo[op][:],
                                        lhsT=woutt_b[:, dc * 1024 + oc * 128:
                                                     dc * 1024 + (oc + 1) * 128],
                                        rhs=y_st[gp][:, dc * GRP:(dc + 1) * GRP],
                                        start=(dc == 0), stop=(dc == DC - 1))
                                mm.then_inc(s_wo, 1)

                @blk.gpsimd
                def _(g):
                    def slab_src(kind, idx):
                        if kind == 0:
                            return gu_h[idx]
                        if kind == 1:
                            return dn_h[idx]
                        return woutt_h[:, idx * 1024:(idx + 1) * 1024]

                    def spills(batch):
                        for i in batch:
                            kind, idx = slabs[i]
                            if kind == 2:
                                continue
                            g.wait_ge(w_cse, i + 1)
                            dst = gu_sp if kind == 0 else dn_sp
                            g.dma_start(out=dst[idx],
                                        in_=bb[i % 8][:]).then_inc(w_sp[i % 8], 16)

                    for c in range(NCH):
                        pp = c % 2
                        if c >= 1:
                            spills(slab_of_chunk[c - 1])
                        for i in slab_of_chunk[c]:
                            if i >= 8:
                                g.wait_ge(w_cse, i - 7)
                            g.dma_start(out=fb[i % 8][:],
                                        in_=slab_src(*slabs[i])
                                        ).then_inc(w_in[i % 8], 16)
                        g.wait_ge(s_dtc, c + 1)
                        g.wait_ge(s_xn[pp], 16 * (c // 2 + 1))
                        if c >= 2:
                            g.wait_ge(s_vec, c - 1)
                        g.tensor_mul(out=zc[pp][:], in0=dtc[pp][:],
                                     in1=xnc[pp][:]).then_inc(s_zc, 1)
                        if c >= 1:
                            g.wait_ge(s_da, c)
                        g.tensor_tensor(out=bass.AP(_t(dAarg), 0,
                                                    [[128 * TC, 128], [16 * TC, 8],
                                                     [TC, 16], [1, TC]]),
                                        in0=bass.AP(_t(a_lane), 0,
                                                    [[128, 128], [16, 8], [1, 16],
                                                     [0, TC]]),
                                        in1=bass.AP(_t(dtc[pp]), 0,
                                                    [[8 * TC, 128], [TC, 8],
                                                     [0, 16], [1, TC]]),
                                        op=ALU.mult).then_inc(s_dt2, 1)
                    spills(slab_of_chunk[NCH - 1])
                    for q in range(8):
                        g.wait_ge(w_sp[q], 16 * _spq[q])

                @blk.scalar
                def _(scalar):
                    def softplus(k):
                        kp = k % 2
                        scalar.wait_ge(s_dt1, k + 1)
                        if k >= 2:
                            scalar.wait_ge(s_dtc, k - 1)
                        scalar.activation(out=sp_a[kp][:], in_=dtp[kp][:],
                                          func=AF.Abs)
                        scalar.drain()
                        scalar.activation(out=sp_a[kp][:], in_=sp_a[kp][:],
                                          func=AF.Exp, scale=-1.0)
                        scalar.drain()
                        scalar.activation(out=sp_a[kp][:], in_=sp_a[kp][:],
                                          func=AF.Ln,
                                          bias=c_one[:]).then_inc(s_sp, 1)

                    def casts(batch):
                        for i in batch:
                            kind, idx = slabs[i]
                            scalar.wait_ge(w_in[i % 8], 16 * (i // 8 + 1))
                            if kind == 2:
                                scalar.copy(out=woutt_b[:, idx * 1024:(idx + 1) * 1024],
                                            in_=fb[i % 8][:]).then_inc(w_cse, 1)
                            else:
                                if i - 8 in spill_ord:
                                    scalar.wait_ge(w_sp[i % 8],
                                                   16 * spill_ord[i - 8])
                                scalar.copy(out=bb[i % 8][:],
                                            in_=fb[i % 8][:]).then_inc(w_cse, 1)

                    softplus(0)
                    softplus(1)
                    for c in range(NCH):
                        pp = c % 2
                        if c >= 1:
                            casts(slab_of_chunk[c - 1])
                        if c == NCH - 1:
                            casts(slab_of_chunk[c])
                        scalar.wait_ge(s_dt2, c + 1)
                        if c >= 2:
                            scalar.wait_ge(s_vec, c - 1)
                        scalar.activation(out=lane_ap(dAb[pp], 1),
                                          in_=bass.AP(_t(dAarg), 0,
                                                      [[128 * TC, 128],
                                                       [16 * TC, 8], [TC, 16],
                                                       [1, TC]]),
                                          func=AF.Exp).then_inc(s_da, 1)
                        for gi in ([c // 16 - 1] if (c % 16 == 1 and c > 16)
                                   else ([NG - 1] if c == NCH - 1 else [])):
                            for oc in range(DC):
                                op = oc % 2
                                n = gi * 8 + oc
                                scalar.wait_ge(s_wo, n + 1)
                                if gi >= 1:
                                    scalar.wait_ge(s_sst[oc], 16 * gi)
                                scalar.activation(out=s_tile[oc][:],
                                                  in_=ps_wo[op][:], func=AF.Identity,
                                                  bias=bout[:, oc:oc + 1]
                                                  ).then_inc(s_sev, 1)
                        if c + 2 < NCH:
                            softplus(c + 2)

                @blk.vector
                def _(vector):
                    vector.memset(slot0_ap(dAb[0]), 0.0)
                    vector.memset(slot0_ap(dAb[1]), 0.0)
                    vector.memset(slot0_ap(zB), 0.0)
                    vector.drain()

                    def make_dtp(k):
                        kp = k % 2
                        vector.wait_ge(s_ur[k % 4], 16 * (k // 4 + 1))
                        if k >= 2:
                            vector.wait_ge(s_sp, k - 1)
                            vector.wait_ge(s_dtc, k - 1)
                        for gg in range(DC):
                            ts = vector.tensor_scalar(
                                out=dtp[kp][:, gg * TC:(gg + 1) * TC],
                                in0=urep[k % 4][:],
                                scalar1=wdt[:, gg:gg + 1],
                                scalar2=bdt[:, gg:gg + 1],
                                op0=ALU.mult, op1=ALU.add)
                        ts.then_inc(s_dt1, 1)

                    def make_dtc(k):
                        kp = k % 2
                        vector.wait_ge(s_sp, k + 1)
                        if k >= 2:
                            vector.wait_ge(s_dt2, k - 1)
                        vector.scalar_tensor_tensor(out=dtc[kp][:], in0=dtp[kp][:],
                                                    scalar=0.0, in1=sp_a[kp][:],
                                                    op0=ALU.max,
                                                    op1=ALU.add).then_inc(s_dtc, 1)

                    make_dtp(0)
                    make_dtp(1)
                    make_dtc(0)
                    for c in range(NCH):
                        pp = c % 2
                        gi = c // 16
                        gp = gi % 2
                        if c + 1 < NCH:
                            make_dtc(c + 1)
                        if c + 2 < NCH:
                            make_dtp(c + 2)
                        if c > 0:
                            vector.tensor_copy(out=slot0_ap(zB),
                                               in_=slot0_ap(hB, TC))
                        vector.wait_ge(s_bc, c + 1)
                        vector.wait_ge(s_zc, c + 1)
                        if c >= 1:
                            vector.wait_ge(s_y, c)
                        vector.tensor_tensor(out=lane_ap(zB, 1),
                                             in0=cmp_ap(zc[pp]),
                                             in1=bass.AP(_t(ps_b[pp]), 0,
                                                         [[NS * TC, 128], [0, 8],
                                                          [TC, 16], [1, TC]]),
                                             op=ALU.mult)
                        vector.drain()
                        vector.wait_ge(s_da, c + 1)
                        vector.tensor_tensor_scan(out=flat_ap(hB),
                                                  data0=flat_ap(dAb[pp]),
                                                  data1=flat_ap(zB), initial=0.0,
                                                  op0=ALU.mult, op1=ALU.add)
                        vector.drain()
                        vector.tensor_tensor(out=lane_ap(zB, 1),
                                             in0=lane_ap(hB, 1),
                                             in1=bass.AP(_t(ps_c[pp]), 0,
                                                         [[NS * TC, 128], [0, 8],
                                                          [TC, 16], [1, TC]]),
                                             op=ALU.mult)
                        vector.drain()
                        if gi >= 2 and c % 16 == 0:
                            vector.wait_ge(s_wo, 8 * (gi - 1))
                        with nc.allow_low_precision(
                                reason="y feeds a bf16 matmul operand; the DVE "
                                       "reduce accumulates in fp32 internally"):
                            vector.tensor_reduce(
                                out=bass.AP(_t(y_st[gp]), (c % 16) * TC,
                                            [[DC * GRP, 128], [GRP, 8], [1, TC]]),
                                in_=bass.AP(_t(zB), 1,
                                            [[128 * LN, 128], [16 * LN, 8], [1, TC],
                                             [LN, 16]]),
                                axis=AX.X,
                                op=ALU.add).then_inc(s_y, 1)
                        vector.nop().then_inc(s_vec, 1)
                    if debug:
                        vector.wait_ge(s_sst[0], 16 * NG * DC // 2)
                        vector.wait_ge(s_sst[1], 16 * NG * DC // 2)

                if debug:
                    @blk.gpsimd
                    def _(g):
                        g.wait_ge(s_sst[0], 16 * NG * DC // 2)
                        g.wait_ge(s_sst[1], 16 * NG * DC // 2)
                        g.dma_start(out=dbg["s"],
                                    in_=bass.AP(_t(s_s), 0,
                                                [[T, DC * 128], [1, T]])
                                    ).then_inc(s_dbg, 16)
                        g.dma_start(out=dbg["u"], in_=u_s).then_inc(s_dbg, 16)
                        g.dma_start(out=dbg["xln"],
                                    in_=bass.AP(_t(xln_s), 0,
                                                [[T, DC * 128], [1, T]])
                                    ).then_inc(s_dbg, 16)
                        g.dma_start(out=dbg["y"],
                                    in_=y_st[1][:]).then_inc(s_dbg, 16)
                        g.wait_ge(s_dbg, 16 * 4)

        if phases < 5:
            return nc
        with ExitStack() as st3:
            NWS_SEM = 3
            sm3 = lambda n: st3.enter_context(nc.semaphore(n))
            f_xln = [sm3("f_xln%d" % q) for q in range(2)]
            f_gw = [sm3("f_gw%d" % q) for q in range(NWS_SEM)]
            f_uw = [sm3("f_uw%d" % q) for q in range(NWS_SEM)]
            f_xo = [sm3("f_xo%d" % q) for q in range(2)]
            f_dw, f_mm, f_cl = sm3("f_dw"), sm3("f_mm"), sm3("f_cl")
            f_si, f_g8, f_dn = sm3("f_si"), sm3("f_g8"), sm3("f_dn")
            f_ac = sm3("f_ac")
            f_stm = [sm3("f_stm%d" % q) for q in range(2)]
            f_gs, f_out = sm3("f_gs"), sm3("f_out")
            f_ost = [sm3("f_ost%d" % q) for q in range(4)]
            blk = st3.enter_context(nc.Block())
            sb3 = lambda n, s, d: st3.enter_context(nc.sbuf_tensor(n, s, d))
            xln_g = [sb3("xln_g%d" % i, [128, DC * GRP], BF16) for i in range(2)]
            NWS = 3
            NPS = 3
            gw = [sb3("gw%d" % i, [128, 1024], BF16) for i in range(NWS)]
            uw = [sb3("uw%d" % i, [128, 1024], BF16) for i in range(NWS)]
            wdn = sb3("wdn", [128, 32 * 1024], BF16)
            g8 = [sb3("g8_%d" % i, [128, 32 * GRP], BF16) for i in range(2)]
            acc = [sb3("acc%d" % i, [128, D], F32) for i in range(4)]
            clp = [sb3("clp%d" % i, [128, GRP], F32) for i in range(NPS)]
            sil = [sb3("sil%d" % i, [128, GRP], F32) for i in range(NPS)]
            gt = [sb3("gt%d" % i, [128, D], BF16) for i in range(2)]
            s_tm = [sb3("s_tm%d" % i, [128, D], BF16) for i in range(2)]
            xo = [sb3("xo%d" % i, [128, D], F32) for i in range(2)]
            ps_g = [st3.enter_context(nc.psum_tensor("ps_g%d" % i, [128, GRP], F32))
                    for i in range(NPS)]
            ps_u = [st3.enter_context(nc.psum_tensor("ps_u%d" % i, [128, GRP], F32))
                    for i in range(NPS)]
            ps_dn = st3.enter_context(nc.psum_tensor("ps_dn", [128, D], F32))

            def jj_of(gi, ei):
                return gi * JJ_PER_G + sum(EXPERTS[e][1] for e in range(ei))

            def dnslab_of(gi, ei):
                return gi * DN_PER_G + sum(EXPERTS[e][3] for e in range(ei))

            def comb_cnt(gi, ei):
                return gi * E + min(ei, E - 1) + 1

            @blk.sync
            def _(sync):
                wpend = []

                def wdn_block():
                    e9, dnb, ndc = wpend.pop(0)
                    if e9 >= 1:
                        sync.wait_ge(f_dn, 4 * e9)
                    for j in range(ndc):
                        sync.dma_start(out=wdn[:, j * 1024:(j + 1) * 1024],
                                       in_=dn_sp[dnb + j]).then_inc(f_dw, 16)

                for gi in range(NG):
                    if gi >= 2:
                        sync.wait_ge(f_mm, (gi - 1) * JJ_PER_G)
                    for cc in range(DC):
                        sync.dma_start(
                            out=xln_g[gi % 2][:, cc * GRP:(cc + 1) * GRP],
                            in_=bass.AP(_t(xln_s), cc * 128 * T + gi * GRP,
                                        [[T, 128], [1, GRP]])
                            ).then_inc(f_xln[gi % 2], 16)
                    for ei, (gub, nf, dnb, ndc, shared) in enumerate(EXPERTS):
                        jj0 = jj_of(gi, ei)
                        for j in range(nf):
                            jj = jj0 + j
                            if jj >= 3:
                                sync.wait_ge(f_mm, jj - 2)
                            sync.dma_start(out=gw[jj % 3][:],
                                           in_=gu_sp[gub + j]
                                           ).then_inc(f_gw[jj % 3], 16)
                            if j == 3 and wpend:
                                wdn_block()
                        wpend.append((gi * 9 + ei, dnb, ndc))
                while wpend:
                    wdn_block()

            @blk.gpsimd
            def _(g):
                def g_xo(gc, tb):
                    n = gc * 4 + tb
                    if n >= 2:
                        g.wait_ge(f_out, n - 1)
                    g.dma_start(out=xo[tb % 2][:],
                                in_=x_h[gc * GRP + tb * 128:
                                        gc * GRP + (tb + 1) * 128, :]
                                ).then_inc(f_xo[tb % 2], 16)

                def g_store(gc, tb):
                    g.wait_ge(f_out, gc * 4 + tb + 1)
                    g.dma_start(out=out_h[gc * GRP + tb * 128:
                                          gc * GRP + (tb + 1) * 128, :],
                                in_=acc[tb][:]).then_inc(f_ost[tb], 16)

                for gi in range(NG):
                    for ei, (gub, nf, dnb, ndc, shared) in enumerate(EXPERTS):
                        jj0 = jj_of(gi, ei)
                        xslots = ({2: 0, 4: 1, 13: 2, 18: 3}
                                  if (ei == 0 and gi >= 1) else {})
                        stslots = ({12: 0, 18: 1, 25: 2, 31: 3}
                                   if (ei == 0 and gi >= 1) else {})
                        for j in range(nf):
                            jj = jj0 + j
                            if jj >= 3:
                                g.wait_ge(f_mm, jj - 2)
                            g.dma_start(out=uw[jj % 3][:],
                                        in_=gu_sp[gub + nf + j]
                                        ).then_inc(f_uw[jj % 3], 16)
                            if j in xslots:
                                g_xo(gi - 1, xslots[j])
                            if j in stslots:
                                g_store(gi - 1, stslots[j])
                for tb in range(4):
                    g_xo(NG - 1, tb)
                    g_store(NG - 1, tb)
                for tbq in range(4):
                    g.wait_ge(f_ost[tbq], 16 * NG)

            @blk.tensor
            def _(tensor):
                pend = []

                def emit_down(tensor):
                    e9, ei, jj_end, dn_end, tb = pend.pop(0)
                    di = e9 * 4 + tb
                    tensor.wait_ge(f_g8, jj_end)
                    tensor.wait_ge(f_dw, 16 * dn_end)
                    if di >= 1:
                        tensor.wait_ge(f_ac, di)
                    for j in range(EXPERTS[ei][3]):
                        tensor.matmul(out=ps_dn[:, 0:512],
                                      lhsT=g8[e9 % 2][:, j * GRP + tb * 128:
                                                      j * GRP + (tb + 1) * 128],
                                      rhs=wdn[:, j * 1024:j * 1024 + 512],
                                      start=(j == 0), stop=(j == EXPERTS[ei][3] - 1))
                    for j in range(EXPERTS[ei][3]):
                        mm = tensor.matmul(
                            out=ps_dn[:, 512:1024],
                            lhsT=g8[e9 % 2][:, j * GRP + tb * 128:
                                            j * GRP + (tb + 1) * 128],
                            rhs=wdn[:, j * 1024 + 512:(j + 1) * 1024],
                            start=(j == 0), stop=(j == EXPERTS[ei][3] - 1))
                    mm.then_inc(f_dn, 1)

                for gi in range(NG):
                    gp = gi % 2
                    tensor.wait_ge(f_xln[gi % 2], 16 * DC * (gi // 2 + 1))
                    for ei, (gub, nf, dnb, ndc, shared) in enumerate(EXPERTS):
                        jj0 = jj_of(gi, ei)
                        slots = {min(nf - 1, max(1, (nf * (t + 1)) // 5) + 2): t
                                 for t in range(4)}
                        for j in range(nf):
                            jj = jj0 + j
                            pp = jj % NPS
                            tensor.wait_ge(f_gw[jj % 3], 16 * (jj // 3 + 1))
                            tensor.wait_ge(f_uw[jj % 3], 16 * (jj // 3 + 1))
                            if jj >= NPS:
                                tensor.wait_ge(f_g8, jj - NPS + 1)
                            for cc in range(DC):
                                tensor.matmul(out=ps_g[pp][:],
                                              lhsT=gw[pp][:, cc * 128:(cc + 1) * 128],
                                              rhs=xln_g[gp][:, cc * GRP:(cc + 1) * GRP],
                                              start=(cc == 0), stop=(cc == DC - 1))
                            for cc in range(DC):
                                mm = tensor.matmul(
                                    out=ps_u[pp][:],
                                    lhsT=uw[pp][:, cc * 128:(cc + 1) * 128],
                                    rhs=xln_g[gp][:, cc * GRP:(cc + 1) * GRP],
                                    start=(cc == 0), stop=(cc == DC - 1))
                            mm.then_inc(f_mm, 1)
                            if j in slots and pend:
                                emit_down(tensor)
                        e9 = gi * 9 + ei
                        while pend:
                            emit_down(tensor)
                        for tb in range(4):
                            pend.append((e9, ei, jj0 + nf,
                                         dnslab_of(gi, ei) + ndc, tb))
                while pend:
                    emit_down(tensor)

            @blk.vector
            def _(vector):
                def v_combine(gc, tb):
                    nn = gc * 4 + tb
                    vector.wait_ge(f_gs, nn + 1)
                    vector.wait_ge(f_xo[tb % 2], 16 * (nn // 2 + 1))
                    vector.wait_ge(f_stm[tb % 2], 16 * 8 * (nn // 2 + 1))
                    vector.tensor_mul(out=acc[tb][:], in0=s_tm[tb % 2][:],
                                      in1=gt[tb % 2][:])
                    vector.drain()
                    vector.tensor_add(out=acc[tb][:], in0=acc[tb][:],
                                      in1=xo[tb % 2][:]).then_inc(f_out, 1)

                vpend = []

                def v_evac(di, first):
                    e9, tb = di // 4, di % 4
                    gi, ei = e9 // 9, e9 % 9
                    cb = (comb_sb[:, (gi * 4 + tb) * 8 + ei:
                                  (gi * 4 + tb) * 8 + ei + 1]
                          if ei < E else None)
                    if tb == 0 and not first:
                        vector.drain()
                    vector.wait_ge(f_dn, di + 1)
                    if first:
                        gacc = di // 36
                        if gacc >= 1:
                            vector.wait_ge(f_ost[tb], 16 * gacc)
                        vector.tensor_scalar_mul(out=acc[tb][:], in0=ps_dn[:],
                                                 scalar1=cb).then_inc(f_ac, 1)
                    elif ei < E:
                        vector.scalar_tensor_tensor(out=acc[tb][:], in0=ps_dn[:],
                                                    scalar=cb, in1=acc[tb][:],
                                                    op0=ALU.mult,
                                                    op1=ALU.add).then_inc(f_ac, 1)
                    else:
                        vector.tensor_add(out=acc[tb][:], in0=acc[tb][:],
                                          in1=ps_dn[:]).then_inc(f_ac, 1)

                for gi in range(NG):
                    for ei, (gub, nf, dnb, ndc, shared) in enumerate(EXPERTS):
                        e9 = gi * 9 + ei
                        jj0 = jj_of(gi, ei)
                        cslots = ({12: 0, 18: 1, 25: 2, 31: 3}
                                  if (ei == 0 and gi >= 1) else {})
                        eslots = {min(nf - 1, max(2, (nf * (t + 1)) // 5) + 3): t
                                  for t in range(4)}
                        for j in range(nf):
                            jj = jj0 + j
                            pp = jj % NPS
                            vector.wait_ge(f_mm, jj + 1)
                            vector.tensor_scalar(out=clp[pp][:], in0=ps_g[pp][:],
                                                 scalar1=10.0, scalar2=-10.0,
                                                 op0=ALU.min,
                                                 op1=ALU.max).then_inc(f_cl, 1)
                            vector.wait_ge(f_si, jj + 1)
                            if j == 0 and e9 >= 2:
                                vector.wait_ge(f_dn, 4 * (e9 - 1))
                            vector.tensor_mul(out=g8[e9 % 2][:, j * GRP:(j + 1) * GRP],
                                              in0=sil[pp][:],
                                              in1=ps_u[pp][:]).then_inc(f_g8, 1)
                            if j in cslots:
                                v_combine(gi - 1, cslots[j])
                            if j in eslots and vpend:
                                v_evac(*vpend.pop(0))
                        while vpend:
                            v_evac(*vpend.pop(0))
                        for tb in range(4):
                            vpend.append((e9 * 4 + tb, ei == 0))
                while vpend:
                    v_evac(*vpend.pop(0))
                for tb in range(4):
                    v_combine(NG - 1, tb)
                for tbq in range(4):
                    vector.wait_ge(f_ost[tbq], 16 * NG)

            @blk.scalar
            def _(scalar):
                def s_combine(gc, tb):
                    nn = gc * 4 + tb
                    if nn >= 2:
                        scalar.wait_ge(f_out, nn - 1)
                    for oc in range(DC):
                        scalar.dma_start_transpose(
                            out=s_tm[tb % 2][:, oc * 128:(oc + 1) * 128],
                            in_=bass.AP(_t(s_s),
                                        oc * 128 * T + gc * GRP + tb * 128,
                                        [[T, 128], [1, 128]])
                        ).then_inc(f_stm[tb % 2], 16)
                    scalar.wait_ge(f_ac, gc * 36 + 32 + tb + 1)
                    scalar.activation(out=gt[tb % 2][:], in_=acc[tb][:],
                                      func=AF.Silu).then_inc(f_gs, 1)

                for gi in range(NG):
                    for ei, (gub, nf, dnb, ndc, shared) in enumerate(EXPERTS):
                        jj0 = jj_of(gi, ei)
                        sslots = ({10: 0, 16: 1, 23: 2, 29: 3}
                                  if (ei == 0 and gi >= 1) else {})
                        for j in range(nf):
                            jj = jj0 + j
                            pp = jj % NPS
                            scalar.wait_ge(f_cl, jj + 1)
                            if jj >= NPS:
                                scalar.wait_ge(f_g8, jj - NPS + 1)
                            scalar.activation(out=sil[pp][:], in_=clp[pp][:],
                                              func=AF.Silu).then_inc(f_si, 1)
                            if j in sslots:
                                s_combine(gi - 1, sslots[j])
                for tb in range(4):
                    s_combine(NG - 1, tb)

    return nc


def _prep_inputs(ins):
    f32 = np.float32
    c = np.ascontiguousarray
    A_log = np.asarray(ins["L0_attn_A_log"], f32)
    W_dt = np.asarray(ins["L0_attn_W_dt"], f32)
    b_dt = np.asarray(ins["L0_attn_b_dt"], f32)
    W_xp = np.asarray(ins["L0_attn_W_xp"], f32)
    b_xp = np.asarray(ins["L0_attn_b_xp"], f32)
    W_out = np.asarray(ins["L0_attn_W_out"], f32)
    b_out = np.asarray(ins["L0_attn_b_out"], f32)
    gam = np.asarray(ins["L0_ffn_ln_gamma"], f32)
    bet = np.asarray(ins["L0_ffn_ln_beta"], f32)
    rw = np.asarray(ins["L0_ffn_router_w"], f32)
    gu = np.asarray(ins["L0_ffn_gate_up_w"], f32)
    dn = np.asarray(ins["L0_ffn_down_w"], f32)
    sgu = np.asarray(ins["L0_ffn_shared_gate_up_w"], f32)
    sdn = np.asarray(ins["L0_ffn_shared_down_w"], f32)

    m = {}
    m["a_lane"] = c(A_log.reshape(8, 128, 16).transpose(1, 0, 2).reshape(128, 128))
    m["wdt_fm"] = c(W_dt[:, 0].reshape(8, 128).T)
    m["bdt_fm"] = c(b_dt.reshape(8, 128).T)
    m["wxpt_fm"] = c(W_xp.T.reshape(8, 128, 32).transpose(1, 0, 2).reshape(128, 256))
    m["bxp_col"] = c(b_xp.reshape(32, 1))
    m["woutt_fm"] = c(W_out.T.reshape(8, 128, 1024).transpose(1, 0, 2)
                      .reshape(128, 8192))
    m["bout_fm"] = c(b_out.reshape(8, 128).T)
    m["gam_rep"] = c(np.broadcast_to(gam, (128, D)))
    m["bet_rep"] = c(np.broadcast_to(bet, (128, D)))
    m["rw_fm"] = c(rw.reshape(8, 128, 8).transpose(1, 0, 2).reshape(128, 64))
    gut = gu.reshape(E, 8, 128, 64, 128).transpose(0, 3, 2, 1, 4) \
            .reshape(E * 64, 128, 1024)
    sgut = sgu.reshape(8, 128, 16, 128).transpose(2, 1, 0, 3).reshape(16, 128, 1024)
    m["gu_tiled"] = c(np.concatenate([gut, sgut], axis=0))
    dnt = dn.reshape(E * 32, 128, 1024)
    sdnt = sdn.reshape(8, 128, 1024)
    m["dn_tiled"] = c(np.concatenate([dnt, sdnt], axis=0))
    m["ident_b"] = np.eye(128, dtype=BF)
    m["ident_f"] = np.eye(128, dtype=np.float32)
    m["ones_f"] = np.ones((1, 128), dtype=np.float32)
    return m


def kernel(**inputs):
    x = np.asarray(inputs["L0_attn_x"], np.float32)
    base = _prep_inputs(inputs)
    nc = build_nc(debug=False)
    in_maps = []
    for core in range(8):
        im = dict(base)
        im["x"] = np.ascontiguousarray(x[core])
        in_maps.append(im)
    res = run_bass_kernel_spmd(nc, in_maps, core_ids=list(range(8)))
    return np.stack([np.asarray(r["out"]) for r in res.results], axis=0)
